# version 03
# baseline (speedup 1.0000x reference)
import math
from contextlib import ExitStack

import numpy as np
import ml_dtypes

import concourse.bass as bass
import concourse.mybir as mybir
from concourse.bass_utils import run_bass_kernel_spmd
from concourse.ap import AP

F32 = mybir.dt.float32
BF16 = mybir.dt.bfloat16
F16 = mybir.dt.float16
U8 = mybir.dt.uint8
I32 = mybir.dt.int32
AF = mybir.ActivationFunctionType
ALU = mybir.AluOpType
AX = mybir.AxisListType

NCORES = 8
S = 8192
D = 4096
T = 1024
NT = 8
NKC = 32
EPS = 1e-5
M0 = 6.0
M1 = 8.0
WINC = 150.0
INTER = 11008
ENGINES = ("sync", "tensor", "vector", "scalar", "gpsimd")
bf16 = ml_dtypes.bfloat16


def _alibi_slopes(n=32):
    cp2 = 2 ** int(math.floor(math.log2(n)))
    base = 2.0 ** (-2.0 ** (-(math.log2(cp2) - 3)))
    return (base ** np.arange(1, cp2 + 1, dtype=np.float64)).astype(np.float32)


SLOPES = _alibi_slopes(32)
SCALE1 = 128.0 ** -0.5



def stripe(x):
    xt = x.reshape(64, 128, *x.shape[1:])
    return [np.ascontiguousarray(xt[r::8].reshape(1024, *x.shape[1:]))
            for r in range(NCORES)]


def unstripe(parts):
    out = np.empty((64, 128) + parts[0].shape[1:], dtype=parts[0].dtype)
    for r in range(NCORES):
        out[r::8] = parts[r].reshape(8, 128, *parts[0].shape[1:])
    return out.reshape(8192, *parts[0].shape[1:])


def pack_bytes(qweight):
    R, N = qweight.shape
    b = np.ascontiguousarray(qweight).view(np.uint8).reshape(R, N, 4)
    return np.ascontiguousarray(b.transpose(0, 2, 1).reshape(R * 4, N))



class Plan:
    def __init__(self, nc, stack):
        self.nc = nc
        self.stack = stack
        self.phases = []
        self.cur = None
        self.sem = {}
        self.cnt = {}
        self.new_phase()

    def new_phase(self):
        self.cur = {e: [] for e in ENGINES}
        self.phases.append(self.cur)

    def S(self, name):
        if name not in self.sem:
            self.sem[name] = self.stack.enter_context(self.nc.semaphore(name))
            self.cnt[name] = 0
        return self.sem[name]

    def val(self, name):
        self.S(name)
        return self.cnt[name]

    def bump(self, name, k):
        self.S(name)
        self.cnt[name] += k
        return self.cnt[name]

    def op(self, eng, fn):
        self.cur[eng].append(fn)

    def dma(self, eng, out, in_, sem=None, **kw):
        if sem is None:
            self.op(eng, lambda E: E.dma_start(out=out, in_=in_, **kw))
            return None
        if eng == "gpsimd":
            self.wait(eng, sem, self.val(sem))
        v = self.bump(sem, 16)
        h = self.S(sem)
        self.op(eng, lambda E: E.dma_start(out=out, in_=in_, **kw).then_inc(h, 16))
        return v

    def wait(self, eng, sem, v):
        if v <= 0:
            return
        h = self.S(sem)
        self.op(eng, lambda E: E.wait_ge(h, v))

    def run(self, eng, fn, sem=None, k=1):
        if sem is None:
            self.op(eng, fn)
            return None
        v = self.bump(sem, k)
        h = self.S(sem)
        self.op(eng, lambda E: fn(E).then_inc(h, k))
        return v

    def seq(self, eng, fns, sem):
        h = self.S(sem)
        v = None
        for i, f in enumerate(fns):
            if v is not None:
                self.op(eng, lambda E, v=v: E.wait_ge(h, v))
            v = self.bump(sem, 1)
            self.op(eng, lambda E, f=f: f(E).then_inc(h, 1))
        return v

    def emit(self):
        for ph in self.phases:
            if not any(ph[e] for e in ENGINES):
                continue
            with self.nc.Block() as block:
                for e in ENGINES:
                    ops = ph[e]
                    if not ops:
                        continue
                    getattr(block, e)(lambda E, ops=ops: [f(E) for f in ops])


def sbt(stack, nc, name, shape, dtype):
    return stack.enter_context(nc.sbuf_tensor(name, shape, dtype))


def pst(stack, nc, name, shape, dtype):
    return stack.enter_context(nc.psum_tensor(name, shape, dtype))


def brow_dram(t_ap, offset, n, reps=128):
    return AP(t_ap.tensor, offset, [[0, reps], [1, n]])



def emit_ln(p, x_dram, xnfm, ident, B, tag):
    x2, xntok, sqscr, st, ps2 = B["x2"], B["xntok"], B["sqscr"], B["st"], B["ps2"]
    eps_col = B["eps_col"]
    ld, stg, tk, tr, ev = (tag + s for s in ("ld", "st", "tk", "tr", "ev"))
    sums = st[:, 0:1]
    sumsq = st[:, 1:2]
    m = st[:, 2:3]
    msq = st[:, 3:4]
    var = st[:, 4:5]
    lnv = st[:, 5:6]
    rstd = st[:, 6:7]
    for j in range(NT):
        b = 0
        if j >= 1:
            p.wait("sync", tk, j)
        vld = p.dma("sync", x2[b][:], x_dram[128 * j:128 * j + 128, :], sem=ld)
        p.wait("vector", ld, vld)
        vsu = p.run("vector", lambda v, b=b: v.reduce_sum(sums, x2[b][:], axis=AX.X),
                    sem=stg)
        p.wait("scalar", ld, vld)
        p.wait("scalar", stg, vsu)
        vsq = p.run("scalar", lambda sc, b=b: sc.activation(
            sqscr[:], x2[b][:], AF.Square, accum_out=sumsq), sem=stg)
        p.wait("vector", stg, vsq)

        vva = p.seq("vector", [
            lambda v: v.tensor_scalar(m, sums, 1.0 / D, None, ALU.mult),
            lambda v: v.tensor_scalar(var, sumsq, 1.0 / D, None, ALU.mult),
            lambda v: v.tensor_tensor(msq, m, m, ALU.mult),
            lambda v: v.tensor_tensor(var, var, msq, ALU.subtract),
        ], stg)
        p.wait("scalar", stg, vva)

        vrs = p.seq("scalar", [
            lambda sc: sc.activation(lnv, var, AF.Ln, bias=eps_col),
            lambda sc: sc.activation(rstd, lnv, AF.Exp, scale=-0.5),
        ], stg)
        p.wait("vector", stg, vrs)
        if j >= 1:
            p.wait("vector", tr, j * NKC)
        vtk = p.run("vector", lambda v, b=b: v.tensor_scalar(
            xntok[:], x2[b][:], m, rstd, ALU.subtract, ALU.mult), sem=tk)
        p.wait("tensor", tk, vtk)
        for c in range(NKC):
            if j * NKC + c >= 2:
                p.wait("tensor", ev, j * NKC + c - 1)
            vtr = p.run("tensor", lambda t, c=c: t.transpose(
                ps2[c % 2][:].bitcast(BF16)[:, 0:128],
                xntok[:, 128 * c:128 * c + 128], ident[:]), sem=tr)
            p.wait("vector", tr, vtr)
            p.run("vector", lambda v, c=c, j=j: v.tensor_copy(
                xnfm[:, T * c + 128 * j:T * c + 128 * j + 128],
                ps2[c % 2][:].bitcast(BF16)[:, 0:128]), sem=ev)



def build_launch_A(maxphase=99):
    nc = bass.Bass("TRN2", target_bir_lowering=False)
    dr = lambda n, shp, dt, kind="ExternalInput": nc.dram_tensor(n, shp, dt, kind=kind).ap()
    x_d = dr("x", [T, D], F32)
    wqkv_d = dr("wqkv", [D, 3072], F32)
    bqkv_d = dr("bqkv", [3072], F32)
    w2_d = dr("w2", [D, D], F32)
    b_d = dr("bffn", [D], F32)
    wqT_d = dr("wqT", [D, 256], F32)
    wkT_d = dr("wkT", [D, 256], F32)
    ars_d = dr("ars", [D, 128], F32)
    b2_d = dr("b2", [128, D], F32)
    ident_d = dr("ident", [128, 128], BF16)
    shm_d = dr("shm", [128, 2], U8)
    gB_d = dr("gB", [2048, 1376], U8)
    uB_d = dr("uB", [2048, 1376], U8)
    dB_d = dr("dB", [5504, 512], U8)
    gS_d = dr("gS", [32, 1376], F16)
    uS_d = dr("uS", [32, 1376], F16)
    dS_d = dr("dS", [86, 512], F16)
    gZ_d = dr("gZ", [32, 172], I32)
    uZ_d = dr("uZ", [32, 172], I32)
    dZ_d = dr("dZ", [86, 64], I32)
    zsh_d = dr("zsh", [128, 8], F32)
    zmk_d = dr("zmk", [128, 1], I32)

    q0_d = dr("q0", [1024, T], BF16, "ExternalOutput")
    k0_d = dr("k0", [T, 1024], BF16, "ExternalOutput")
    v0_d = dr("v0", [T, 1024], BF16, "ExternalOutput")
    r0p_d = dr("r0p", [T, D], F32, "ExternalOutput")
    wg_d = dr("wgsl", [D, 1376], BF16, "ExternalOutput")
    wu_d = dr("wusl", [D, 1376], BF16, "ExternalOutput")
    wd_d = dr("wdsl", [INTER, 512], BF16, "ExternalOutput")
    zpg_scr = dr("zpgscr", [32, 1376], F32, "Internal")
    zpu_scr = dr("zpuscr", [32, 1376], F32, "Internal")
    zpd_scr = dr("zpdscr", [86, 512], F32, "Internal")

    with ExitStack() as stack:
        p = Plan(nc, stack)
        x2 = [sbt(stack, nc, "x0sb", [128, D], F32)]
        xntok = sbt(stack, nc, "xntok", [128, D], BF16)
        sqscr = xntok
        st = sbt(stack, nc, "st", [128, 16], F32)
        xnfm = sbt(stack, nc, "xnfm", [128, NKC * T], BF16)
        ident = sbt(stack, nc, "identsb", [128, 128], BF16)
        wbf = [sbt(stack, nc, f"wbf{i}", [128, 4096], BF16) for i in range(2)]
        qstage = [sbt(stack, nc, f"qs{i}", [128, 512], BF16) for i in range(2)]
        biasq = sbt(stack, nc, "biasq", [128, 8], F32)
        biaskv = sbt(stack, nc, "biaskv", [128, 2048], F32)
        ostage = [sbt(stack, nc, f"os{i}", [128, 512], F32) for i in range(2)]
        xrl = [sbt(stack, nc, f"xrl{i}", [128, 512], F32) for i in range(2)]
        bfb = [sbt(stack, nc, f"bfb{i}", [128, 512], F32) for i in range(2)]
        wmix = sbt(stack, nc, "wmix", [128, NKC * 640], BF16)
        b2b = sbt(stack, nc, "b2b", [128, D], BF16)
        mixs = sbt(stack, nc, "mixs", [128, 1024], BF16)
        mixw = sbt(stack, nc, "mixw", [128, 64], F32)
        qk = sbt(stack, nc, "qk", [128, 256], F32)
        hwfm = sbt(stack, nc, "hwfm", [128, T], BF16)
        pp = [pst(stack, nc, f"pp{i}", [128, 1024], F32) for i in range(2)]
        ps_hi = [pst(stack, nc, f"ps{i}", [128, 512], F32) for i in range(4, 8)]
        ps = [pp[0][:, 0:512], pp[0][:, 512:1024],
              pp[1][:, 0:512], pp[1][:, 512:1024]] + ps_hi

        dq_u = [sbt(stack, nc, f"dqu{i}", [128, 512], U8) for i in range(2)]
        dq_sr = [sbt(stack, nc, f"dqsr{i}", [128, 512], F32) for i in range(2)]
        dq_zr = [sbt(stack, nc, f"dqzr{i}", [128, 512], F32) for i in range(2)]
        dq_t = [sbt(stack, nc, f"dqt{i}", [128, 512], F32) for i in range(2)]
        dq_w = [sbt(stack, nc, f"dqw{i}", [128, 512], BF16) for i in range(2)]
        dq_s = sbt(stack, nc, "dqs", [128, 1376], F32)
        dq_zp = sbt(stack, nc, "dqzp", [128, 1376], F32)
        dq_zi = biaskv[:].bitcast(I32)
        dq_shm = sbt(stack, nc, "dqshm", [128, 2], U8)
        dq_zsh = sbt(stack, nc, "dqzsh", [128, 8], F32)
        dq_zmk = sbt(stack, nc, "dqzmk", [128, 1], I32)

        cst = sbt(stack, nc, "cstsb", [128, 4], F32)
        p.run("vector", lambda v: v.memset(cst[:, 0:1], EPS))
        p.dma("sync", ident[:], ident_d[:], sem="c0")
        for mm in range(8):
            p.dma("sync", biasq[:, mm:mm + 1],
                  AP(bqkv_d.tensor, 128 * mm, [[1, 128], [1, 1]]), sem="c0")
        p.dma("sync", biaskv[:], brow_dram(bqkv_d, 1024, 2048), sem="c0")
        p.dma("gpsimd", b2b[:], b2_d[:], sem="c0g")
        p.dma("gpsimd",
              AP(wmix.ap().tensor, 0, [[NKC * 640, 128], [640, NKC], [1, 256]]),
              AP(wqT_d.tensor, 0, [[256, 128], [128 * 256, NKC], [1, 256]]), sem="c0g")
        p.dma("gpsimd",
              AP(wmix.ap().tensor, 256, [[NKC * 640, 128], [640, NKC], [1, 256]]),
              AP(wkT_d.tensor, 0, [[256, 128], [128 * 256, NKC], [1, 256]]), sem="c0g")
        p.dma("gpsimd",
              AP(wmix.ap().tensor, 512, [[NKC * 640, 128], [640, NKC], [1, 128]]),
              AP(ars_d.tensor, 0, [[128, 128], [128 * 128, NKC], [1, 128]]), sem="c0g")
        p.dma("sync", dq_shm[:], shm_d[:], sem="c0")
        p.dma("sync", dq_zsh[:], zsh_d[:], sem="c0")
        p.dma("sync", dq_zmk[:], zmk_d[:], sem="c0")
        c0tot = p.val("c0")
        c0gtot = p.val("c0g")
        p.new_phase()
        for _e in ("tensor", "vector", "scalar"):
            p.wait(_e, "c0", c0tot)
            p.wait(_e, "c0g", c0gtot)

        if maxphase < 1:
            p.emit()
            return nc
        emit_ln(p, x_d, xnfm, ident,
                dict(x2=x2, xntok=xntok, sqscr=sqscr, st=st, ps2=ps[6:8],
                     eps_col=cst[:, 0:1]), "ln")
        p.new_phase()

        if maxphase < 2:
            p.emit()
            return nc
        for mm in range(8):
            b = mm % 2
            if mm >= 2:
                p.wait("gpsimd", "p2mm", 2 * mm - 2)
            vw = p.dma("gpsimd", wbf[b][:],
                       AP(wqkv_d.tensor, 128 * mm,
                          [[3072, 128], [128 * 3072, NKC], [1, 128]]), sem=f"p2w{b}")
            for tb in range(2):
                idx = 2 * mm + tb
                pj = ps[idx % 4]
                p.wait("tensor", f"p2w{b}", vw)
                if idx >= 4:
                    p.wait("tensor", "p2ev", idx - 3)

                def fmm(t, pj=pj, b=b, tb=tb):
                    for kc in range(NKC):
                        r = t.matmul(pj[:], wbf[b][:, 128 * kc:128 * kc + 128],
                                     xnfm[:, T * kc + 512 * tb:T * kc + 512 * tb + 512],
                                     start=(kc == 0), stop=(kc == NKC - 1))
                    return r
                vm = p.run("tensor", fmm, sem="p2mm")
                stg = qstage[idx % 2]
                p.wait("vector", "p2mm", vm)
                if idx >= 2:
                    p.wait("vector", f"p2st{idx % 2}", p.val(f"p2st{idx % 2}"))
                vev = p.run("vector", lambda v, pj=pj, stg=stg, mm=mm: v.tensor_scalar(
                    stg[:], pj[:], biasq[:, mm:mm + 1], None, ALU.add), sem="p2ev")
                p.wait("gpsimd", "p2ev", vev)
                p.dma("gpsimd", q0_d[128 * mm:128 * mm + 128, 512 * tb:512 * tb + 512],
                      stg[:], sem=f"p2st{idx % 2}")
        p.new_phase()

        if maxphase < 3:
            for s_ in ("p2st",):
                p.wait("gpsimd", s_, p.val(s_))
            p.emit()
            return nc
        for _b in range(2):
            p.wait("vector", f"p2st{_b}", p.val(f"p2st{_b}"))
        for cb in range(4):
            out_d = k0_d if cb < 2 else v0_d
            ocol = (cb % 2) * 512
            for kc in range(NKC):
                b = kc % 2
                gidx = cb * NKC + kc
                if gidx >= 2:
                    p.wait("gpsimd", "p3mm", gidx - 1)
                vw = p.dma("gpsimd", wbf[b][:, 0:512],
                           AP(wqkv_d.tensor, 3072 * 128 * kc + 1024 + 512 * cb,
                              [[3072, 128], [1, 512]]), sem=f"p3w{b}")
                p.wait("tensor", f"p3w{b}", vw)
                if kc == 0 and cb > 0:
                    p.wait("tensor", "p3ev", 8 * cb)

                def fmm3(t, b=b, kc=kc):
                    for tok in range(NT):
                        r = t.matmul(ps[tok][:],
                                     xnfm[:, T * kc + 128 * tok:T * kc + 128 * tok + 128],
                                     wbf[b][:, 0:512],
                                     start=(kc == 0), stop=(kc == NKC - 1))
                    return r
                p.run("tensor", fmm3, sem="p3mm")
            vmm = p.val("p3mm")
            for tok in range(NT):
                sidx = cb * NT + tok
                p.wait("vector", "p3mm", vmm)
                if sidx >= 2:
                    p.wait("vector", f"p3st{sidx % 2}", p.val(f"p3st{sidx % 2}"))
                stg = qstage[sidx % 2]
                vev = p.run("vector", lambda v, tok=tok, cb=cb, stg=stg: v.tensor_tensor(
                    stg[:], ps[tok][:],
                    biaskv[:, 512 * cb:512 * cb + 512], ALU.add), sem="p3ev")
                p.wait("gpsimd", "p3ev", vev)
                p.dma("gpsimd", out_d[128 * tok:128 * tok + 128, ocol:ocol + 512],
                      stg[:], sem=f"p3st{sidx % 2}")
        p.new_phase()

        if maxphase < 4:
            for s_ in ("p2st", "p3st"):
                p.wait("gpsimd", s_, p.val(s_))
            p.emit()
            return nc
        qf = mixs[:, 0:256]
        kf = mixs[:, 256:512]
        hf3 = AP(mixs.ap().tensor, 512, [[1024, 128], [16, 8], [1, 16]])
        hw3 = AP(mixs.ap().tensor, 640, [[1024, 128], [16, 8], [1, 16]])
        hw2 = mixs[:, 640:768]
        scv = mixw[:, 0:8]
        negmax = mixw[:, 8:9]
        negmax_s = mixw[:, 9:10]
        expw = mixw[:, 16:24]
        sumw = mixw[:, 24:25]
        rzw = mixw[:, 25:26]
        wv = mixw[:, 32:40]
        wv3 = AP(mixw.ap().tensor, 32, [[64, 128], [1, 8], [0, 16]])
        for tok in range(NT):
            if tok >= 1:
                p.wait("tensor", "p4tr", tok)

            def fmm4(t, tok=tok):
                for kc in range(NKC):
                    xs = xnfm[:, T * kc + 128 * tok:T * kc + 128 * tok + 128]
                    t.matmul(ps[0][:, 0:256], xs, wmix[:, 640 * kc:640 * kc + 256],
                             start=(kc == 0), stop=(kc == NKC - 1))
                    t.matmul(ps[1][:, 0:256], xs, wmix[:, 640 * kc + 256:640 * kc + 512],
                             start=(kc == 0), stop=(kc == NKC - 1))
                    r = t.matmul(ps[2][:, 0:128], xs,
                                 wmix[:, 640 * kc + 512:640 * kc + 640],
                                 start=(kc == 0), stop=(kc == NKC - 1))
                return r
            vm = p.run("tensor", fmm4, sem="p4mm")
            p.wait("vector", "p4mm", vm)

            vv = p.seq("vector", [
                lambda v: v.tensor_copy(qf, ps[0][:, 0:256]),
                lambda v: v.tensor_copy(kf, ps[1][:, 0:256]),
                lambda v: v.tensor_copy(mixs[:, 512:640], ps[2][:, 0:128]),
                lambda v: v.tensor_tensor(qk[:], qf, kf, ALU.mult),
                lambda v: v.reduce_sum(scv, AP(qk.ap().tensor, 0,
                                               [[256, 128], [32, 8], [1, 32]]),
                                       axis=AX.X),
                lambda v: v.reduce_max(negmax, scv, axis=AX.X, negate=True),
                lambda v: v.tensor_scalar(negmax_s, negmax, 1.0 / math.sqrt(32.0),
                                          None, ALU.mult),
            ], "p4v")
            p.wait("scalar", "p4v", vv)
            vsc = p.run("scalar", lambda sc: sc.activation(
                expw, scv, AF.Exp, bias=negmax_s, scale=1.0 / math.sqrt(32.0),
                accum_out=sumw), sem="p4s")
            p.wait("vector", "p4s", vsc)

            vw4 = p.seq("vector", [
                lambda v: v.reciprocal(rzw, sumw),
                lambda v: v.tensor_scalar(wv, expw, rzw, None, ALU.mult),
                lambda v: v.tensor_tensor(hw3, hf3, wv3, ALU.mult),
            ], "p4w")
            p.wait("tensor", "p4w", vw4)
            vtr = p.run("tensor", lambda t: t.transpose(
                ps[4][:].bitcast(BF16)[:, 0:128], hw2, ident[:]), sem="p4tr0")
            p.wait("vector", "p4tr0", vtr)
            p.run("vector", lambda v, tok=tok: v.tensor_copy(
                hwfm[:, 128 * tok:128 * tok + 128], ps[4][:].bitcast(BF16)[:, 0:128]),
                sem="p4tr")
        p.new_phase()

        if maxphase < 5:
            for s_ in ("p2st", "p3st"):
                p.wait("gpsimd", s_, p.val(s_))
            p.emit()
            return nc
        def dq_prep(tag, S_d, Z_d, ZP_d, G, N, prev=None):
            if prev is not None:
                for e in ("sync", "gpsimd"):
                    p.wait(e, prev + "zp", p.val(prev + "zp"))
                    p.wait(e, prev + "zw", p.val(prev + "zw"))
            vs = p.dma("gpsimd", dq_s[0:G, 0:N], S_d[:], sem=tag + "s")
            vz = p.dma("sync", dq_zi[0:G, 0:N // 8], Z_d[:], sem=tag + "z")
            p.wait("vector", tag + "s", vs)
            p.wait("vector", tag + "z", vz)

            def fz_unpack(v, G=G, N=N):
                for i in range(8):
                    r = v.tensor_scalar(
                        AP(dq_zi.tensor, 176 + i, [[2048, G], [8, N // 8]]),
                        dq_zi[0:G, 0:N // 8],
                        4 * i, 15,
                        ALU.logical_shift_right, ALU.bitwise_and)
                return r
            vzp = p.seq("vector", [
                fz_unpack,
                lambda v, G=G, N=N: v.scalar_tensor_tensor(
                    dq_zp[0:G, 0:N],
                    AP(dq_zi.tensor, 176, [[2048, G], [1, N]]),
                    1.0, dq_s[0:G, 0:N], ALU.add, ALU.mult),
            ], tag + "zp")
            p.wait("gpsimd", tag + "zp", vzp)
            vzw = p.dma("gpsimd", ZP_d[:], dq_zp[0:G, 0:N], sem=tag + "zw")
            p.wait("scalar", tag + "zw", vzw)
            p.wait("gpsimd", tag + "zw", vzw)

        wmix_i32 = wmix[:].bitcast(I32)
        dq_qa = [wmix_i32[:, 0:512], wmix_i32[:, 512:1024]]
        dq_qb = [wmix_i32[:, 1024:1536], wmix_i32[:, 1536:2048]]

        def dq_tile_emitters():
            tiles = []
            state = {"ti": 0}
            gu_blocks = [(0, 512), (512, 512), (1024, 352)]
            for tag, B_d, S_d, ZP_d, W_d, G, N, nblocks in (
                    ("dg", gB_d, gS_d, zpg_scr, wg_d, 32, 1376, gu_blocks),
                    ("du", uB_d, uS_d, zpu_scr, wu_d, 32, 1376, gu_blocks),
                    ("dd", dB_d, dS_d, zpd_scr, wd_d, 86, 512, [(0, 512)])):
                for kc in range(G):
                    for (nb0, nbw) in nblocks:
                        def emit(tag=tag, B_d=B_d, S_d=S_d, ZP_d=ZP_d, W_d=W_d,
                                 G=G, N=N, kc=kc, nb0=nb0, nbw=nbw):
                            ti = state["ti"]
                            state["ti"] += 1
                            b = ti % 2
                            if ti >= 2:
                                p.wait("sync", "dqq", 5 * (ti - 1))
                                p.wait("scalar", "dqq", 5 * (ti - 1))
                            vu = p.dma("sync", dq_u[b][:, 0:nbw],
                                       AP(B_d.tensor, (64 * kc) * N + nb0,
                                          [[0, 2], [N, 64], [1, nbw]]),
                                       sem=f"dqu{b}")
                            vsr = p.dma("gpsimd", dq_sr[b][:, 0:nbw],
                                        brow_dram(S_d, kc * N + nb0, nbw),
                                        sem=f"dqsr{b}")
                            vzr = p.dma("scalar", dq_zr[b][:, 0:nbw],
                                        brow_dram(ZP_d, kc * N + nb0, nbw),
                                        sem=f"dqzr{b}")
                            p.wait("vector", f"dqu{b}", vu)
                            p.wait("vector", f"dqsr{b}", vsr)
                            p.wait("vector", f"dqzr{b}", vzr)
                            if ti >= 2:
                                p.wait("vector", f"dqw{ti % 2}",
                                       p.val(f"dqw{ti % 2}"))
                            vq = p.seq("vector", [
                                lambda v, b=b, nbw=nbw: v.tensor_copy(
                                    dq_qa[b][:, 0:nbw], dq_u[b][:, 0:nbw]),
                                lambda v, b=b, nbw=nbw: v.tensor_single_scalar(
                                    dq_qb[b][0:64, 0:nbw], dq_qa[b][0:64, 0:nbw],
                                    15, ALU.bitwise_and),
                                lambda v, b=b, nbw=nbw: v.tensor_scalar(
                                    dq_qb[b][64:128, 0:nbw], dq_qa[b][64:128, 0:nbw],
                                    4, 15, ALU.logical_shift_right, ALU.bitwise_and),
                                lambda v, b=b, nbw=nbw: v.tensor_tensor(
                                    dq_t[b][:, 0:nbw], dq_qb[b][:, 0:nbw],
                                    dq_sr[b][:, 0:nbw], ALU.mult),
                                lambda v, b=b, nbw=nbw: v.tensor_tensor(
                                    dq_w[b][:, 0:nbw], dq_t[b][:, 0:nbw],
                                    dq_zr[b][:, 0:nbw], ALU.subtract),
                            ], "dqq")
                            p.wait("gpsimd", "dqq", vq)
                            p.dma("gpsimd",
                                  AP(W_d.tensor, (128 * kc) * N + nb0,
                                     [[2 * N, 64], [1, nbw]]),
                                  dq_w[b][0:64, 0:nbw], sem=f"dqw{ti % 2}")
                            p.dma("gpsimd",
                                  AP(W_d.tensor, (128 * kc + 1) * N + nb0,
                                     [[2 * N, 64], [1, nbw]]),
                                  dq_w[b][64:128, 0:nbw], sem=f"dqw{ti % 2}")
                        tiles.append(emit)
            return tiles

        wf32 = [x2[0][:, 0:512], x2[0][:, 512:1024]]
        dq_prep("dg", gS_d, gZ_d, zpg_scr, 32, 1376)
        dq_prep("du", uS_d, uZ_d, zpu_scr, 32, 1376, prev="dg")
        dq_prep("dd", dS_d, dZ_d, zpd_scr, 86, 512, prev="du")
        dq_tiles = dq_tile_emitters()
        ndq = len(dq_tiles)
        dqi = 0
        for ob in range(8):
            bb = bfb[ob % 2]
            if ob >= 2:
                p.wait("sync", "p5ev", 16 * (ob - 1))
            vb = p.dma("sync", bb[:], brow_dram(b_d, 512 * ob, 512), sem=f"p5b{ob % 2}")
            for kc in range(NKC):
                b = kc % 2
                gidx = ob * (NKC + 1) + kc
                if gidx >= 2:
                    p.wait("sync", "p5mm", p.val("p5mm") - 1)
                vwr = p.dma("sync", wf32[b],
                            AP(w2_d.tensor, D * 128 * kc + 512 * ob,
                               [[D, 128], [1, 512]]), sem=f"p5wr{b}")
                p.wait("scalar", f"p5wr{b}", vwr)
                if gidx >= 2:
                    p.wait("scalar", "p5mm", p.val("p5mm") - 1)
                vw = p.run("scalar", lambda sc, b=b: sc.activation(
                    wbf[b][:, 0:512], wf32[b], AF.Copy), sem="p5wc")
                p.wait("tensor", "p5wc", vw)
                if kc == 0 and ob > 0:
                    p.wait("tensor", "p5ev", 16 * ob)

                def fmm5(t, b=b, kc=kc):
                    for tok in range(NT):
                        r = t.matmul(ps[tok][:],
                                     xnfm[:, T * kc + 128 * tok:T * kc + 128 * tok + 128],
                                     wbf[b][:, 0:512], start=(kc == 0), stop=False)
                    return r
                p.run("tensor", fmm5, sem="p5mm")
                take = (ndq * (ob * NKC + kc + 1)) // (8 * NKC) - dqi
                for _ in range(take):
                    dq_tiles[dqi]()
                    dqi += 1

            def fmm5b(t, ob=ob):
                for tok in range(NT):
                    r = t.matmul(ps[tok][:], hwfm[:, 128 * tok:128 * tok + 128],
                                 b2b[:, 512 * ob:512 * ob + 512], start=False, stop=True)
                return r
            vm5 = p.run("tensor", fmm5b, sem="p5mm")
            for tok in range(NT):
                sidx = ob * NT + tok
                bx = xrl[tok % 2]
                if sidx >= 2:
                    p.wait("sync", "p5ev", 2 * (sidx - 1))
                vx = p.dma("sync", bx[:],
                           x_d[128 * tok:128 * tok + 128, 512 * ob:512 * ob + 512],
                           sem=f"p5x{tok % 2}")
                os_b = ostage[tok % 2]
                p.wait("vector", "p5mm", vm5)
                p.wait("vector", f"p5b{ob % 2}", vb)
                p.wait("vector", f"p5x{tok % 2}", vx)
                if sidx >= 2:
                    p.wait("vector", f"p5st{sidx % 2}", p.val(f"p5st{sidx % 2}"))

                vev = p.seq("vector", [
                    lambda v, tok=tok, bb=bb, os_b=os_b:
                        v.tensor_tensor(os_b[:], ps[tok][:], bb[:], ALU.add),
                    lambda v, bx=bx, os_b=os_b:
                        v.tensor_tensor(os_b[:], os_b[:], bx[:], ALU.add),
                ], "p5ev")
                p.wait("gpsimd", "p5ev", vev)
                p.dma("gpsimd", r0p_d[128 * tok:128 * tok + 128, 512 * ob:512 * ob + 512],
                      os_b[:], sem=f"p5st{sidx % 2}")
        p.new_phase()

        p.new_phase()
        for sem_name in ("p2st0", "p2st1", "p3st0", "p3st1", "p5st0", "p5st1",
                         "dqw0", "dqw1"):
            p.wait("gpsimd", sem_name, p.val(sem_name))
        p.emit()
    return nc


def prep_launch_A(inputs):
    x = np.ascontiguousarray(inputs["L0_attn_x"][0])
    xs = stripe(x)
    w2 = np.ascontiguousarray(inputs["L0_ffn_W"].T)
    wqT = np.ascontiguousarray(inputs["L0_ffn_Wq"].T)
    wkT = np.ascontiguousarray(inputs["L0_ffn_Wk"].T)
    ars = np.ascontiguousarray(inputs["L0_ffn_A"].reshape(128, D).T)
    b2 = np.ascontiguousarray(inputs["L0_ffn_Bm"].transpose(0, 2, 1).reshape(128, D))
    ident = np.eye(128, dtype=bf16)
    shm = np.stack([(4 * (np.arange(128) % 2)).astype(np.uint8),
                    np.full(128, 15, np.uint8)], axis=1)
    zsh = np.zeros((128, 8), np.float32)
    zmk = np.full((128, 1), 15, np.int32)
    gB = pack_bytes(inputs["L1_ffn_gate_qweight"])
    uB = pack_bytes(inputs["L1_ffn_up_qweight"])
    dB = pack_bytes(inputs["L1_ffn_down_qweight"])
    maps = []
    for r in range(NCORES):
        n0, n1 = 1376 * r, 1376 * (r + 1)
        c0, c1 = 512 * r, 512 * (r + 1)
        maps.append({
            "x": xs[r],
            "wqkv": inputs["L0_attn_Wqkv"], "bqkv": inputs["L0_attn_bqkv"],
            "w2": w2, "bffn": inputs["L0_ffn_b"],
            "wqT": wqT, "wkT": wkT, "ars": ars, "b2": b2,
            "ident": ident, "shm": shm, "zsh": zsh, "zmk": zmk,
            "gB": np.ascontiguousarray(gB[:, n0:n1]),
            "uB": np.ascontiguousarray(uB[:, n0:n1]),
            "dB": np.ascontiguousarray(dB[:, c0:c1]),
            "gS": np.ascontiguousarray(inputs["L1_ffn_gate_scales"][:, n0:n1]),
            "uS": np.ascontiguousarray(inputs["L1_ffn_up_scales"][:, n0:n1]),
            "dS": np.ascontiguousarray(inputs["L1_ffn_down_scales"][:, c0:c1]),
            "gZ": np.ascontiguousarray(inputs["L1_ffn_gate_zeros"][:, 172 * r:172 * (r + 1)]),
            "uZ": np.ascontiguousarray(inputs["L1_ffn_up_zeros"][:, 172 * r:172 * (r + 1)]),
            "dZ": np.ascontiguousarray(inputs["L1_ffn_down_zeros"][:, 64 * r:64 * (r + 1)]),
        })
    return maps





def build_launch_B():
    nc = bass.Bass("TRN2", target_bir_lowering=False)
    dr = lambda n, shp, dt, kind="ExternalInput": nc.dram_tensor(n, shp, dt, kind=kind).ap()
    q0_d = dr("q0", [1024, T], BF16)
    k0f_d = dr("k0fT", [1024, S], BF16)
    v0f_d = dr("v0f", [S, 1024], BF16)
    r0p_d = dr("r0p", [T, D], F32)
    wo_d = dr("wo", [1024, D], F32)
    bo_d = dr("bo", [D], F32)
    wqkv1_d = dr("wqkv1", [D, 4352], F32)
    ident_d = dr("ident", [128, 128], BF16)
    x1_d = dr("x1", [T, D], F32, "ExternalOutput")
    xn1_d = dr("xn1", [D, T], BF16, "ExternalOutput")
    q1_d = dr("q1", [D, T], BF16, "ExternalOutput")
    k1_d = dr("k1", [T, 128], BF16, "ExternalOutput")
    v1_d = dr("v1", [T, 128], BF16, "ExternalOutput")

    with ExitStack() as stack:
        p = Plan(nc, stack)
        big = sbt(stack, nc, "big", [128, NKC * T], BF16)
        q0sb = sbt(stack, nc, "q0sb", [128, 8 * T], BF16)
        ofm = sbt(stack, nc, "ofm", [128, D], BF16)
        kfmt = [sbt(stack, nc, f"kfmt{i}", [128, 1024], BF16) for i in range(2)]
        vt = [sbt(stack, nc, f"vt{i}", [128, 1024], BF16) for i in range(2)]
        wbf = [sbt(stack, nc, f"wbf{i}", [128, 4096], BF16) for i in range(2)]
        stg4 = [sbt(stack, nc, f"stg{i}", [128, 512], F32) for i in range(2)]
        bstg = [sbt(stack, nc, f"bstg{i}", [128, 512], BF16) for i in range(2)]
        bob = [sbt(stack, nc, f"bob{i}", [128, 512], F32) for i in range(2)]
        rpl = [sbt(stack, nc, f"rpl{i}", [128, 512], F32) for i in range(2)]
        zsb = sbt(stack, nc, "zsb", [128, 16], F32)
        zrow = sbt(stack, nc, "zrow", [128, 512], F32)
        cst = sbt(stack, nc, "cstsb", [128, 4], F32)
        ones_col = sbt(stack, nc, "onescol", [128, 1], BF16)
        ident = sbt(stack, nc, "identsb", [128, 128], BF16)
        x2 = [sbt(stack, nc, "x2sb", [128, D], F32)]
        st = sbt(stack, nc, "stsb", [128, 16], F32)
        pp = [pst(stack, nc, f"pp{i}", [128, 1024], F32) for i in range(2)]
        ps_hi = [pst(stack, nc, f"ps{i}", [128, 512], F32) for i in range(4, 8)]
        ps = [pp[0][:, 0:512], pp[0][:, 512:1024],
              pp[1][:, 0:512], pp[1][:, 512:1024]] + ps_hi

        p.run("vector", lambda v: v.memset(cst[:, 0:1], EPS))
        p.run("vector", lambda v: v.memset(cst[:, 1:2], -M0))
        p.run("vector", lambda v: v.memset(ones_col[:], 1.0))
        p.dma("sync", ident[:], ident_d[:], sem="c0")
        p.dma("sync", q0sb[:],
              AP(q0_d.tensor, 0, [[1024, 128], [128 * 1024, 8], [1, 1024]]), sem="c0")
        c0tot = p.val("c0")
        p.new_phase()
        for _e in ("tensor", "vector", "scalar"):
            p.wait(_e, "c0", c0tot)

        for tb in range(2):
            for tk in range(64):
                kb = tk % 2
                if tk >= 2:
                    p.wait("sync", "b1mm", p.val("b1mm") - 1)
                vk = p.dma("sync", kfmt[kb][:],
                           AP(k0f_d.tensor, 128 * tk,
                              [[S, 128], [128 * S, 8], [1, 128]]),
                           sem=f"b1k{kb}")
                stps = ps[tk % 2]
                p.wait("tensor", f"b1k{kb}", vk)
                if tk >= 2:
                    p.wait("tensor", "b1x", p.val("b1x") - 1)

                def fst(t, stps=stps, kb=kb, tb=tb):
                    for dc in range(8):
                        r = t.matmul(stps[:], kfmt[kb][:, 128 * dc:128 * dc + 128],
                                     q0sb[:, 1024 * dc + 512 * tb:1024 * dc + 512 * tb + 512],
                                     start=(dc == 0), stop=(dc == 7))
                    return r
                vm = p.run("tensor", fst, sem="b1mm")
                p.wait("scalar", "b1mm", vm)
                vx = p.run("scalar", lambda sc, stps=stps, tk=tk: sc.activation(
                    big[:, 512 * tk:512 * tk + 512], stps[:], AF.Exp,
                    bias=cst[:, 1:2], scale=1.0 / 32.0), sem="b1x")
                p.wait("tensor", "b1x", vx)
                p.run("tensor", lambda t, tk=tk: t.matmul(
                    ps[2][0:1, :], ones_col[:], big[:, 512 * tk:512 * tk + 512],
                    start=(tk == 0), stop=(tk == 63)), sem="b1z")
            vz = p.val("b1z")
            p.wait("vector", "b1z", vz)
            vzr = p.seq("vector", [
                lambda v: v.tensor_copy(zrow[0:1, :], ps[2][0:1, :]),
                lambda v: v.reciprocal(zrow[32:33, :], zrow[0:1, :]),
            ], "b1zr")
            for tc in range(4):
                p.wait("sync", "b1zr", vzr)
                p.dma("sync", zsb[0:128, 8 + tc:9 + tc],
                      AP(zrow.ap().tensor, 32 * 512 + 128 * tc, [[512, 1], [1, 128]]),
                      sem="b1zc")
            vzc = p.val("b1zc")
            p.wait("vector", "b1zc", vzc)
            p.new_phase()
            if tb > 0:
                p.wait("tensor", "b2ev", p.val("b2ev"))
            for tk in range(64):
                vb = tk % 2
                if tk >= 2:
                    p.wait("sync", "b2mm", p.val("b2mm") - 1)
                vv = p.dma("sync", vt[vb][:],
                           v0f_d[128 * tk:128 * tk + 128, :], sem=f"b2v{vb}")
                p.wait("tensor", f"b2v{vb}", vv)

                def fpv(t, vb=vb, tk=tk):
                    for dc in range(8):
                        r = t.matmul(ps[dc][:], vt[vb][:, 128 * dc:128 * dc + 128],
                                     big[:, 512 * tk:512 * tk + 512],
                                     start=(tk == 0), stop=(tk == 63))
                    return r
                p.run("tensor", fpv, sem="b2mm")
            vm2 = p.val("b2mm")
            p.wait("vector", "b2mm", vm2)
            for dc in range(8):
                p.run("vector", lambda v, dc=dc: v.tensor_copy(
                    ofm[:, 512 * dc:512 * dc + 512], ps[dc][:]), sem="b2ev")
            p.new_phase()
            for mb in range(8):
                bb = bob[mb % 2]
                vbo = p.dma("sync", bb[:], brow_dram(bo_d, 512 * mb, 512),
                            sem=f"b3b{mb % 2}")
                for dc in range(8):
                    wb = dc % 2
                    gidx = mb * 8 + dc
                    if gidx >= 2:
                        p.wait("gpsimd", "b3mm", p.val("b3mm") - 1)
                    vw = p.dma("gpsimd", wbf[wb][:, 0:512],
                               AP(wo_d.tensor, D * 128 * dc + 512 * mb,
                                  [[D, 128], [1, 512]]), sem=f"b3w{wb}")
                    p.wait("tensor", f"b3w{wb}", vw)
                    if dc == 0 and (mb > 0 or tb > 0):
                        p.wait("tensor", "b3ev", p.val("b3ev"))

                    def fwo(t, wb=wb, dc=dc):
                        for tc in range(4):
                            r = t.matmul(ps[4 + tc][:],
                                         ofm[:, 512 * dc + 128 * tc:512 * dc + 128 * tc + 128],
                                         wbf[wb][:, 0:512], start=(dc == 0), stop=(dc == 7))
                        return r
                    p.run("tensor", fwo, sem="b3mm")
                vm3 = p.val("b3mm")
                for tc in range(4):
                    sidx = (tb * 8 + mb) * 4 + tc
                    tokj = tb * 4 + tc
                    if sidx >= 2:
                        p.wait("sync", "b3ev", p.val("b3ev") - 3)
                    vr = p.dma("sync", rpl[tc % 2][:],
                               r0p_d[128 * tokj:128 * tokj + 128, 512 * mb:512 * mb + 512],
                               sem=f"b3r{tc % 2}")
                    p.wait("vector", "b3mm", vm3)
                    p.wait("vector", f"b3b{mb % 2}", vbo)
                    p.wait("vector", f"b3r{tc % 2}", vr)
                    if sidx >= 2:
                        p.wait("vector", f"b3st{sidx % 2}", p.val(f"b3st{sidx % 2}"))
                    stg = stg4[tc % 2]
                    vev = p.seq("vector", [
                        lambda v, tc=tc, stg=stg: v.tensor_scalar(
                            stg[:], ps[4 + tc][:], zsb[:, 8 + tc:9 + tc], None, ALU.mult),
                        lambda v, stg=stg, bb=bb: v.tensor_tensor(
                            stg[:], stg[:], bb[:], ALU.add),
                        lambda v, stg=stg, tc=tc: v.tensor_tensor(
                            stg[:], stg[:], rpl[tc % 2][:], ALU.add),
                    ], "b3ev")
                    p.wait("gpsimd", "b3ev", vev)
                    p.dma("gpsimd", x1_d[128 * tokj:128 * tokj + 128,
                                         512 * mb:512 * mb + 512],
                          stg[:], sem=f"b3st{sidx % 2}")
            p.new_phase()

        for _b in range(2):
            p.wait("sync", f"b3st{_b}", p.val(f"b3st{_b}"))
        emit_ln(p, x1_d, big, ident,
                dict(x2=x2, xntok=ofm, sqscr=ofm, st=st, ps2=ps[6:8],
                     eps_col=cst[:, 0:1]), "lb")
        for c in range(NKC):
            p.wait("sync", "lbev", p.val("lbev"))
            p.dma("sync", xn1_d[128 * c:128 * c + 128, :],
                  big[:, T * c:T * c + T], sem="bxn")
        p.new_phase()

        for mc in range(NKC):
            b = mc % 2
            if mc >= 2:
                p.wait("gpsimd", "b5mm", 2 * mc - 2)
            vw = p.dma("gpsimd", wbf[b][:],
                       AP(wqkv1_d.tensor, 128 * mc,
                          [[4352, 128], [128 * 4352, NKC], [1, 128]]), sem=f"b5w{b}")
            for tb in range(2):
                idx = 2 * mc + tb
                pj = ps[idx % 4]
                p.wait("tensor", f"b5w{b}", vw)
                if idx >= 4:
                    p.wait("tensor", "b5ev", idx - 3)

                def fq1(t, pj=pj, b=b, tb=tb):
                    for kc in range(NKC):
                        r = t.matmul(pj[:], wbf[b][:, 128 * kc:128 * kc + 128],
                                     big[:, T * kc + 512 * tb:T * kc + 512 * tb + 512],
                                     start=(kc == 0), stop=(kc == NKC - 1))
                    return r
                vm = p.run("tensor", fq1, sem="b5mm")
                stg = bstg[idx % 2]
                p.wait("vector", "b5mm", vm)
                if idx >= 2:
                    p.wait("vector", f"b5st{idx % 2}", p.val(f"b5st{idx % 2}"))
                vev = p.run("vector", lambda v, pj=pj, stg=stg: v.tensor_copy(
                    stg[:], pj[:]), sem="b5ev")
                p.wait("gpsimd", "b5ev", vev)
                p.dma("gpsimd", q1_d[128 * mc:128 * mc + 128, 512 * tb:512 * tb + 512],
                      stg[:], sem=f"b5st{idx % 2}")
        for kc in range(NKC):
            b = kc % 2
            if kc >= 2:
                p.wait("gpsimd", "b6mm", kc - 1)
            vw = p.dma("gpsimd", wbf[b][:, 0:256],
                       AP(wqkv1_d.tensor, 4352 * 128 * kc + 4096,
                          [[4352, 128], [1, 256]]), sem=f"b6w{b}")
            p.wait("tensor", f"b6w{b}", vw)
            if kc == 0:
                p.wait("tensor", "b5ev", p.val("b5ev"))

            def fkv(t, b=b, kc=kc):
                for tok in range(NT):
                    r = t.matmul(ps[tok][:, 0:256],
                                 big[:, T * kc + 128 * tok:T * kc + 128 * tok + 128],
                                 wbf[b][:, 0:256], start=(kc == 0), stop=(kc == NKC - 1))
                return r
            p.run("tensor", fkv, sem="b6mm")
        vm6 = p.val("b6mm")
        for tok in range(NT):
            p.wait("vector", "b6mm", vm6)
            if tok >= 2:
                p.wait("vector", f"b6st{tok % 2}", p.val(f"b6st{tok % 2}"))
            stg = bstg[tok % 2]
            vev = p.run("vector", lambda v, tok=tok, stg=stg: v.tensor_copy(
                stg[:, 0:256], ps[tok][:, 0:256]), sem="b6ev")
            p.wait("gpsimd", "b6ev", vev)
            p.dma("gpsimd", k1_d[128 * tok:128 * tok + 128, :],
                  stg[:, 0:128], sem=f"b6st{tok % 2}")
            p.dma("gpsimd", v1_d[128 * tok:128 * tok + 128, :],
                  stg[:, 128:256], sem=f"b6st{tok % 2}")

        p.new_phase()
        for sem_name in ("b3st0", "b3st1", "bxn", "b5st0", "b5st1", "b6st0", "b6st1"):
            p.wait("gpsimd" if sem_name != "bxn" else "sync", sem_name, p.val(sem_name))
        p.emit()
    return nc


def prep_launch_B(inputs, a_results):
    k0f = np.ascontiguousarray(unstripe([np.asarray(r["k0"]) for r in a_results]).T)
    v0f = unstripe([np.asarray(r["v0"]) for r in a_results])
    ident = np.eye(128, dtype=bf16)
    maps = []
    for r in range(NCORES):
        maps.append({
            "q0": a_results[r]["q0"],
            "k0fT": k0f, "v0f": v0f,
            "r0p": a_results[r]["r0p"],
            "wo": inputs["L0_attn_Wo"], "bo": inputs["L0_attn_bo"],
            "wqkv1": inputs["L1_attn_w_qkv"],
            "ident": ident,
        })
    return maps



def attn1_spans():
    NW = []
    for h in range(32):
        Wh = math.ceil(WINC / float(SLOPES[h]))
        NW.append(math.ceil((Wh + 1151) / 512.0) + 1)
    out = {}
    for j in range(NT):
        for hg in range(8):
            cnt = min(2 * j + 2, NW[4 * hg + 3])
            u_hi = 2 * j + 1
            out[(j, hg)] = list(range(u_hi - cnt + 1, u_hi + 1))
    return out


def build_launch_C():
    nc = bass.Bass("TRN2", target_bir_lowering=False)
    dr = lambda n, shp, dt, kind="ExternalInput": nc.dram_tensor(n, shp, dt, kind=kind).ap()
    q1_d = dr("q1", [D, T], BF16)
    k1f_d = dr("k1fT", [128, S], BF16)
    v1f_d = dr("v1f", [S, 128], BF16)
    x1_d = dr("x1", [T, D], F32)
    xn1_d = dr("xn1", [D, T], BF16)
    wg_d = dr("wg", [D, INTER], BF16)
    wu_d = dr("wu", [D, INTER], BF16)
    wd_d = dr("wd", [INTER, D], BF16)
    wc_d = dr("wc", [D, D], F32)
    ramp_d = dr("ramp", [1, 512], F32)
    r2_d = dr("r2rows", [2, S], BF16)
    base_d = dr("basecols", [128, 256], F32)
    maskT_d = dr("maskT", [128, 1024], F32)
    out_d = dr("outp", [T, D], F32, "ExternalOutput")
    inter_scr = dr("interscr", [INTER, T], BF16, "Internal")
    x2p_scr = dr("x2pscr", [T, D], F32, "Internal")
    down_scr = dr("downscr", [D, T], F16, "Internal")

    spans = attn1_spans()
    SC1 = float(SCALE1)

    with ExitStack() as stack:
        p = Plan(nc, stack)
        mega = sbt(stack, nc, "mega", [128, 45056], BF16)
        k1fm = sbt(stack, nc, "k1fm", [128, S], BF16)
        v1sb = sbt(stack, nc, "v1sb", [128, S], BF16)
        q1g = sbt(stack, nc, "q1g", [128, 4 * T], BF16)
        qsc = sbt(stack, nc, "qsc", [128, 4 * T], BF16)
        maskT = sbt(stack, nc, "maskTsb", [128, 1024], F32)
        basec = sbt(stack, nc, "basec", [128, 256], F32)
        rowcol = sbt(stack, nc, "rowcol", [128, 8], F32)
        Psb = [sbt(stack, nc, f"Psb{i}", [128, 1024], BF16) for i in range(2)]
        PTbig = sbt(stack, nc, "PTbig", [128, 4096], BF16)
        ones2 = sbt(stack, nc, "ones2", [128, 512], BF16)
        R2sb = mega[:, 32768:32768 + S]
        zparts = sbt(stack, nc, "zparts", [128, 96], F32)
        zrow = sbt(stack, nc, "zrowc", [128, 512], F32)
        zrec = sbt(stack, nc, "zrecc", [128, 512], F32)
        rzb = sbt(stack, nc, "rzb", [128, 512], F32)
        ones_row = sbt(stack, nc, "onesrow", [128, 128], F32)
        ident = sbt(stack, nc, "identsb", [128, 128], BF16)
        wsb = [sbt(stack, nc, f"wsb{i}", [128, 4096], BF16) for i in range(4)]
        stg = [sbt(stack, nc, f"stgc{i}", [128, 512], F32) for i in range(2)]
        bstg = [sbt(stack, nc, f"bstgc{i}", [128, 1024], BF16) for i in range(2)]
        fstg = [sbt(stack, nc, f"fstgc{i}", [128, 512], F16) for i in range(2)]
        xrl = [sbt(stack, nc, f"xrlc{i}", [128, 512], F32) for i in range(2)]
        pp = [pst(stack, nc, f"pp{i}", [128, 1024], F32) for i in range(2)]
        ps_hi = [pst(stack, nc, f"ps{i}", [128, 512], F32) for i in range(4, 8)]
        ps = [pp[0][:, 0:512], pp[0][:, 512:1024],
              pp[1][:, 0:512], pp[1][:, 512:1024]] + ps_hi

        p.run("vector", lambda v: v.memset(ones_row[0:1, :], 1.0))
        p.run("vector", lambda v: v.memset(ones2[0:2, :], 1.0))
        p.dma("sync", R2sb[0:2, :], r2_d[:], sem="c0")
        p.dma("sync", maskT[:], maskT_d[:], sem="c0")
        p.dma("sync", basec[:], base_d[:], sem="c0")
        p.dma("sync", ident[:], nc.dram_tensor("identd", [128, 128], BF16,
                                               kind="ExternalInput").ap()[:], sem="c0")
        p.dma("sync", k1fm[:], k1f_d[:], sem="c0")
        p.dma("sync", v1sb[:],
              AP(v1f_d.tensor, 0, [[128, 128], [128 * 128, 64], [1, 128]]), sem="c0")
        c0tot = p.val("c0")
        p.new_phase()
        for _e in ("tensor", "vector", "scalar"):
            p.wait(_e, "c0", c0tot)

        spp = 0
        o_ps = ps[7]
        rz_ps = ps[6]
        apt_end = {}
        aex_end = {}
        for hg in range(8):
            vq = p.dma("sync", q1g[:],
                       AP(q1_d.tensor, 512 * hg * T, [[T, 128], [128 * T, 4], [1, T]]),
                       sem=f"aq{hg % 2}")
            p.wait("vector", f"aq{hg % 2}", vq)
            if hg >= 1:
                p.wait("vector", "amm", p.val("amm"))
            for hl in range(4):
                h = 4 * hg + hl
                p.run("vector", lambda v, hl=hl, h=h: v.tensor_scalar(
                    qsc[:, T * hl:T * hl + T], q1g[:, T * hl:T * hl + T],
                    SC1 / float(SLOPES[h]), None, ALU.mult), sem="aqs")
            vqs = p.val("aqs")
            p.wait("tensor", "aqs", vqs)
            for j in range(NT):
                us = spans[(j, hg)]
                nu = len(us)
                p.wait("scalar", "azr", p.val("azr"))
                first_pv = True
                for pi in range(0, nu, 2):
                    ub = us[pi:pi + 2]
                    for h4 in range(4):
                        h = 4 * hg + h4
                        stage = spp % 2
                        pj = pp[stage]
                        if spp >= 2:
                            p.wait("tensor", "aex", aex_end[spp - 2])

                        def fsc(t, pj=pj, h4=h4, j=j, ub=ub):
                            for k, u in enumerate(ub):
                                t.matmul(pj[:, 512 * k:512 * k + 512],
                                         ones2[0:2, 0:128],
                                         R2sb[0:2, 512 * u:512 * u + 512],
                                         start=True, stop=False)
                                r = t.matmul(
                                    pj[:, 512 * k:512 * k + 512],
                                    qsc[:, T * h4 + 128 * j:T * h4 + 128 * j + 128],
                                    k1fm[:, 512 * u:512 * u + 512],
                                    start=False, stop=True)
                            return r
                        vmm = p.run("tensor", fsc, sem="amm")
                        masked = False
                        for k, u in enumerate(ub):
                            if u >= 2 * j:
                                p.wait("vector", "amm", vmm)
                                tt_off = 512 * (u - 2 * j)
                                vmsk = p.run("vector",
                                             lambda v, pj=pj, k=k, tt_off=tt_off:
                                             v.tensor_tensor(
                                                 pj[:, 512 * k:512 * k + 512],
                                                 pj[:, 512 * k:512 * k + 512],
                                                 maskT[:, tt_off:tt_off + 512],
                                                 ALU.add), sem="amsk")
                                masked = True
                        p.wait("scalar", "amm", vmm)
                        if masked:
                            p.wait("scalar", "amsk", p.val("amsk"))
                        if spp >= 2:
                            p.wait("scalar", "apt", apt_end[spp - 2])
                        w2 = 512 * len(ub)
                        zc = zparts[:, (h4 * 10 + pi // 2):(h4 * 10 + pi // 2) + 1]
                        vex = p.run("scalar", lambda sc, pj=pj, spp=spp, h=h, zc=zc,
                                    h4=h4, j=j, w2=w2: sc.activation(
                            Psb[spp % 2][:, 0:w2], pj[:, 0:w2], AF.Exp,
                            scale=float(SLOPES[h]),
                            bias=basec[:, 8 * h + j:8 * h + j + 1],
                            accum_out=zc), sem="aex")
                        aex_end[spp] = vex
                        p.wait("tensor", "aex", vex)
                        trb = ps[4 + spp % 2]
                        p.wait("tensor", "apt", max(0, p.val("apt") - 1))
                        nch = 4 * len(ub)

                        def ftr(t, spp=spp, trb=trb, nch=nch):
                            for c in range(nch):
                                r = t.transpose(
                                    trb[:].bitcast(BF16)[:, 128 * c:128 * c + 128],
                                    Psb[spp % 2][:, 128 * c:128 * c + 128],
                                    ident[:])
                            return r
                        vtr = p.run("tensor", ftr, sem="atr")
                        p.wait("vector", "atr", vtr)
                        if spp >= 2:
                            p.wait("vector", "apv", p.val("apv"))
                        vpt = p.run("vector", lambda v, trb=trb, h4=h4, nch=nch:
                                    v.tensor_copy(
                            AP(PTbig.ap().tensor, 128 * h4,
                               [[4096, 128], [512, nch], [1, 128]]),
                            AP(trb[:].bitcast(BF16).tensor, 0,
                               [[1024, 128], [128, nch], [1, 128]])), sem="apt")
                        apt_end[spp] = vpt
                        spp += 1
                    vpt_all = p.val("apt")
                    p.wait("tensor", "apt", vpt_all)
                    for k, u in enumerate(ub):
                        for c in range(4):
                            tkc = 4 * u + c
                            last = (pi + k == nu - 1) and (c == 3)
                            p.run("tensor", lambda t, tkc=tkc, c=c, k=k,
                                  st=first_pv and k == 0 and c == 0, last=last:
                                  t.matmul(o_ps[:],
                                           v1sb[:, 128 * tkc:128 * tkc + 128],
                                           PTbig[:, 512 * (4 * k + c):512 * (4 * k + c) + 512],
                                           start=st, stop=last), sem="apv")
                    first_pv = False
                vpv = p.val("apv")
                p.wait("vector", "apv", vpv)
                npr = (nu + 1) // 2
                vzr = p.seq("vector", [
                    lambda v, npr=npr: [v.reduce_sum(
                        zparts[:, 90 + h4:91 + h4],
                        zparts[:, 10 * h4:10 * h4 + npr], axis=AX.X)
                        for h4 in range(4)][-1],
                ], "azr")
                p.wait("sync", "azr", vzr)
                for h4 in range(4):
                    p.dma("sync", zrow[0:1, 128 * h4:128 * h4 + 128],
                          AP(zparts.ap().tensor, 90 + h4, [[96, 128], [1, 1]]),
                          sem="azc")
                vzc = p.val("azc")
                p.wait("vector", "azc", vzc)
                vrr = p.seq("vector", [
                    lambda v: v.reciprocal(zrec[0:1, 0:512], zrow[0:1, 0:512]),
                ], "azz")
                p.wait("tensor", "azz", vrr)
                vbm = p.run("tensor", lambda t: t.matmul(
                    rz_ps[:, 0:512], ones_row[0:1, :], zrec[0:1, 0:512],
                    start=True, stop=True), sem="arzm")
                p.wait("vector", "arzm", vbm)
                vev = p.seq("vector", [
                    lambda v: v.tensor_copy(rzb[:], rz_ps[:, 0:512]),
                    lambda v, hg=hg, j=j: v.tensor_tensor(
                        AP(mega.ap().tensor, 128 * (32 * hg + j),
                           [[45056, 128], [1024, 4], [1, 128]]),
                        AP(o_ps.ap().tensor, 0, [[512, 128], [128, 4], [1, 128]]),
                        AP(rzb.ap().tensor, 0, [[512, 128], [128, 4], [1, 128]]),
                        ALU.mult),
                ], "aoe")
                p.wait("tensor", "aoe", p.val("aoe"))
        p.new_phase()

        for mb in range(8):
            for fc in range(NKC):
                gidx = mb * NKC + fc
                wb = gidx % 2
                if gidx >= 2:
                    p.wait("gpsimd", "cwm", p.val("cwm") - 1)
                vw = p.dma("gpsimd", wsb[wb][:, 0:512],
                           AP(wc_d.tensor, D * 128 * fc + 512 * mb,
                              [[D, 128], [1, 512]]), sem=f"cww{wb}")
                p.wait("tensor", f"cww{wb}", vw)
                if fc == 0 and mb > 0:
                    p.wait("tensor", "cwe", p.val("cwe"))

                def fwc(t, wb=wb, fc=fc):
                    for j in range(NT):
                        r = t.matmul(ps[j][:],
                                     mega[:, 128 * (8 * fc + j):128 * (8 * fc + j) + 128],
                                     wsb[wb][:, 0:512], start=(fc == 0),
                                     stop=(fc == NKC - 1))
                    return r
                p.run("tensor", fwc, sem="cwm")
            vwm = p.val("cwm")
            for j in range(NT):
                sidx = mb * NT + j
                if sidx >= 2:
                    p.wait("sync", "cwe", p.val("cwe"))
                vx = p.dma("sync", xrl[j % 2][:],
                           x1_d[128 * j:128 * j + 128, 512 * mb:512 * mb + 512],
                           sem=f"cwx{j % 2}")
                p.wait("vector", "cwm", vwm)
                p.wait("vector", f"cwx{j % 2}", vx)
                if sidx >= 2:
                    p.wait("vector", f"cws{sidx % 2}", p.val(f"cws{sidx % 2}"))
                sg = stg[j % 2]
                vev = p.seq("vector", [
                    lambda v, j=j, sg=sg: v.tensor_tensor(
                        sg[:], ps[j][:], xrl[j % 2][:], ALU.add),
                ], "cwe")
                p.wait("gpsimd", "cwe", vev)
                p.dma("gpsimd", x2p_scr[128 * j:128 * j + 128, 512 * mb:512 * mb + 512],
                      sg[:], sem=f"cws{sidx % 2}")
        p.new_phase()

        vxn = p.dma("sync", mega[:, 0:NKC * T],
                    AP(xn1_d.tensor, 0, [[T, 128], [128 * T, NKC], [1, T]]), sem="cxn")
        p.wait("tensor", "cxn", vxn)
        for ic in range(INTER // 128):
            b = ic % 2
            if ic >= 2:
                p.wait("sync", "cgm", p.val("cgm") - 1)
            vg = p.dma("sync", wsb[b][:],
                       AP(wg_d.tensor, 128 * ic,
                          [[INTER, 128], [128 * INTER, NKC], [1, 128]]), sem=f"cgw{b}")
            vu = p.dma("sync", wsb[2 + b][:],
                       AP(wu_d.tensor, 128 * ic,
                          [[INTER, 128], [128 * INTER, NKC], [1, 128]]), sem=f"cgw{b}")
            pg = ps[0 + 4 * b:2 + 4 * b]
            pu = ps[2 + 4 * b:4 + 4 * b]
            p.wait("tensor", f"cgw{b}", vu)
            if ic >= 2:
                p.wait("tensor", "cge", p.val("cge"))

            def fgu(t, b=b, pg=pg, pu=pu):
                for kc in range(NKC):
                    for tb in range(2):
                        xs = mega[:, T * kc + 512 * tb:T * kc + 512 * tb + 512]
                        t.matmul(pg[tb][:], wsb[b][:, 128 * kc:128 * kc + 128], xs,
                                 start=(kc == 0), stop=(kc == NKC - 1))
                        r = t.matmul(pu[tb][:], wsb[2 + b][:, 128 * kc:128 * kc + 128],
                                     xs, start=(kc == 0), stop=(kc == NKC - 1))
                return r
            vm = p.run("tensor", fgu, sem="cgm")
            for tb in range(2):
                eidx = ic * 2 + tb
                p.wait("scalar", "cgm", vm)
                if eidx >= 2:
                    p.wait("scalar", "cge", p.val("cge") - 1)
                vsl = p.run("scalar", lambda sc, pg=pg, tb=tb, b=b: sc.activation(
                    bstg[tb][:, 0:512], pg[tb][:], AF.Silu), sem="csl")
                p.wait("vector", "csl", vsl)
                if eidx >= 2:
                    p.wait("vector", f"cis{eidx % 2}", p.val(f"cis{eidx % 2}"))
                vev = p.seq("vector", [
                    lambda v, pu=pu, tb=tb: v.tensor_tensor(
                        bstg[tb][:, 512:1024], bstg[tb][:, 0:512], pu[tb][:],
                        ALU.mult),
                ], "cge")
                p.wait("gpsimd", "cge", vev)
                p.dma("gpsimd", inter_scr[128 * ic:128 * ic + 128,
                                          512 * tb:512 * tb + 512],
                      bstg[tb][:, 512:1024], sem=f"cis{eidx % 2}")
        p.new_phase()

        for _b in range(2):
            p.wait("sync", f"cis{_b}", p.val(f"cis{_b}"))
        NIC = INTER // 128
        icb_sizes = [29, 29, 28]
        for th in range(2):
            if th >= 1:
                p.wait("sync", "cdm", p.val("cdm"))
            vint = p.dma("sync", mega[:, 0:NIC * 512],
                         AP(inter_scr.tensor, 512 * th,
                            [[T, 128], [128 * T, NIC], [1, 512]]), sem=f"cil{th}")
            p.wait("tensor", f"cil{th}", p.val(f"cil{th}"))
            for mc in range(NKC):
                icb0 = 0
                gidx = th * NKC + mc
                pd = ps[mc % 2]
                for bi, bsz in enumerate(icb_sizes):
                    wb = (gidx * 3 + bi) % 2
                    if gidx * 3 + bi >= 2:
                        p.wait("sync", "cdm", p.val("cdm") - 1)
                    vw = p.dma("sync", wsb[wb][:, 0:128 * bsz],
                               AP(wd_d.tensor, D * 128 * icb0 + 128 * mc,
                                  [[D, 128], [128 * D, bsz], [1, 128]]),
                               sem=f"cdw{wb}")
                    p.wait("tensor", f"cdw{wb}", vw)
                    if bi == 0 and gidx >= 2:
                        p.wait("tensor", "cde", p.val("cde"))

                    def fdn(t, wb=wb, icb0=icb0, bsz=bsz, pd=pd, bi=bi):
                        for ii in range(bsz):
                            ic = icb0 + ii
                            r = t.matmul(pd[:], wsb[wb][:, 128 * ii:128 * ii + 128],
                                         mega[:, 512 * ic:512 * ic + 512],
                                         start=(ic == 0), stop=(ic == NIC - 1))
                        return r
                    p.run("tensor", fdn, sem="cdm")
                    icb0 += bsz
                vdm = p.val("cdm")
                p.wait("vector", "cdm", vdm)
                if gidx >= 2:
                    p.wait("vector", f"cds{gidx % 2}", p.val(f"cds{gidx % 2}"))
                vev = p.seq("vector", [
                    lambda v, pd=pd, gidx=gidx: v.tensor_copy(
                        fstg[gidx % 2][:], pd[:]),
                ], "cde")
                p.wait("gpsimd", "cde", vev)
                p.dma("gpsimd", down_scr[128 * mc:128 * mc + 128,
                                         512 * th:512 * th + 512],
                      fstg[gidx % 2][:], sem=f"cds{gidx % 2}")
        p.new_phase()

        for _b in range(2):
            p.wait("sync", f"cds{_b}", p.val(f"cds{_b}"))
            p.wait("sync", f"cws{_b}", p.val(f"cws{_b}"))
        for j in range(NT):
            for mb in range(8):
                sidx = j * 8 + mb
                bx = xrl[sidx % 2]
                bd = bstg[sidx % 2][:, 0:512].bitcast(F16)[:, 0:512]
                if sidx >= 2:
                    p.wait("sync", "cfe", p.val("cfe"))
                vx = p.dma("sync", bx[:],
                           x2p_scr[128 * j:128 * j + 128, 512 * mb:512 * mb + 512],
                           sem=f"cfx{sidx % 2}")
                vd = p.dma("sync", bd,
                           down_scr[512 * mb:512 * mb + 512, 128 * j:128 * j + 128],
                           sem=f"cfx{sidx % 2}", transpose=True)
                p.wait("vector", f"cfx{sidx % 2}", vd)
                if sidx >= 2:
                    p.wait("vector", f"cfs{sidx % 2}", p.val(f"cfs{sidx % 2}"))
                sg = stg[sidx % 2]
                vev = p.seq("vector", [
                    lambda v, sg=sg, bx=bx, bd=bd: v.tensor_tensor(
                        sg[:], bx[:], bd, ALU.add),
                ], "cfe")
                p.wait("gpsimd", "cfe", vev)
                p.dma("gpsimd", out_d[128 * j:128 * j + 128, 512 * mb:512 * mb + 512],
                      sg[:], sem=f"cfs{sidx % 2}")
        p.new_phase()
        for _b in range(2):
            p.wait("gpsimd", f"cfs{_b}", p.val(f"cfs{_b}"))
        p.emit()
    return nc


def prep_launch_C(inputs, a_results, b_results):
    k1f = np.ascontiguousarray(unstripe([np.asarray(r["k1"]) for r in b_results]).T)
    v1f = unstripe([r["v1"] for r in b_results])
    wg = np.concatenate([r["wgsl"] for r in a_results], axis=1)
    wu = np.concatenate([r["wusl"] for r in a_results], axis=1)
    wd = np.concatenate([r["wdsl"] for r in a_results], axis=1)
    ident = np.eye(128, dtype=bf16)
    ramp = np.arange(512, dtype=np.float32).reshape(1, 512)
    upr = np.arange(8192)
    r2 = np.stack([(upr % 256).astype(bf16), (upr - upr % 256).astype(bf16)])
    maps = []
    for r in range(NCORES):
        basec = np.zeros((128, 256), np.float32)
        for h in range(32):
            for j in range(NT):
                posq = 128.0 * (r + 8 * j) + np.arange(128, dtype=np.float32)
                basec[:, 8 * h + j] = -float(SLOPES[h]) * posq - M1
        maskT = np.zeros((128, 1024), np.float32)
        for t in range(2):
            pk = 512 * t + np.arange(512, dtype=np.float32)[None, :]
            pq = 128.0 * r + np.arange(128, dtype=np.float32)[:, None]
            maskT[:, 512 * t:512 * t + 512] = np.where(pk <= pq, 0.0, -2e30)
        maps.append({
            "q1": b_results[r]["q1"], "k1fT": k1f, "v1f": v1f,
            "x1": b_results[r]["x1"], "xn1": b_results[r]["xn1"],
            "wg": wg, "wu": wu, "wd": wd, "wc": inputs["L1_attn_w_c"],
            "ramp": ramp, "r2rows": r2, "basecols": basec, "maskT": maskT,
            "identd": ident,
        })
    return maps



def kernel(**inputs):
    mapsA = prep_launch_A(inputs)
    ncA = build_launch_A()
    resA = run_bass_kernel_spmd(ncA, mapsA, core_ids=list(range(NCORES))).results
    mapsB = prep_launch_B(inputs, resA)
    ncB = build_launch_B()
    resB = run_bass_kernel_spmd(ncB, mapsB, core_ids=list(range(NCORES))).results
    mapsC = prep_launch_C(inputs, resA, resB)
    ncC = build_launch_C()
    resC = run_bass_kernel_spmd(ncC, mapsC, core_ids=list(range(NCORES))).results
    out = unstripe([np.asarray(r["outp"], np.float32) for r in resC])
    return out.reshape(1, S, D)


# version 04 (final)
# speedup vs baseline: 1.3113x; 1.3113x over previous
import math
from contextlib import ExitStack

import numpy as np
import ml_dtypes

import concourse.bass as bass
import concourse.mybir as mybir
from concourse.bass_utils import run_bass_kernel_spmd
from concourse.ap import AP

F32 = mybir.dt.float32
BF16 = mybir.dt.bfloat16
F16 = mybir.dt.float16
U8 = mybir.dt.uint8
I32 = mybir.dt.int32
AF = mybir.ActivationFunctionType
ALU = mybir.AluOpType
AX = mybir.AxisListType

NCORES = 8
S = 8192
D = 4096
T = 1024
NT = 8
NKC = 32
EPS = 1e-5
M0 = 6.0
M1 = 8.0
WINC = 150.0
INTER = 11008
ENGINES = ("sync", "tensor", "vector", "scalar", "gpsimd")
bf16 = ml_dtypes.bfloat16


def _alibi_slopes(n=32):
    cp2 = 2 ** int(math.floor(math.log2(n)))
    base = 2.0 ** (-2.0 ** (-(math.log2(cp2) - 3)))
    return (base ** np.arange(1, cp2 + 1, dtype=np.float64)).astype(np.float32)


SLOPES = _alibi_slopes(32)
SCALE1 = 128.0 ** -0.5



def stripe(x):
    xt = x.reshape(64, 128, *x.shape[1:])
    return [np.ascontiguousarray(xt[r::8].reshape(1024, *x.shape[1:]))
            for r in range(NCORES)]


def unstripe(parts):
    out = np.empty((64, 128) + parts[0].shape[1:], dtype=parts[0].dtype)
    for r in range(NCORES):
        out[r::8] = parts[r].reshape(8, 128, *parts[0].shape[1:])
    return out.reshape(8192, *parts[0].shape[1:])


def pack_bytes(qweight):
    R, N = qweight.shape
    b = np.ascontiguousarray(qweight).view(np.uint8).reshape(R, N, 4)
    return np.ascontiguousarray(b.transpose(0, 2, 1).reshape(R * 4, N))



class Plan:
    def __init__(self, nc, stack):
        self.nc = nc
        self.stack = stack
        self.phases = []
        self.cur = None
        self.sem = {}
        self.cnt = {}
        self.new_phase()

    def new_phase(self):
        self.cur = {e: [] for e in ENGINES}
        self.phases.append(self.cur)

    def S(self, name):
        if name not in self.sem:
            self.sem[name] = self.stack.enter_context(self.nc.semaphore(name))
            self.cnt[name] = 0
        return self.sem[name]

    def val(self, name):
        self.S(name)
        return self.cnt[name]

    def bump(self, name, k):
        self.S(name)
        self.cnt[name] += k
        return self.cnt[name]

    def op(self, eng, fn):
        self.cur[eng].append(fn)

    def dma(self, eng, out, in_, sem=None, **kw):
        if sem is None:
            self.op(eng, lambda E: E.dma_start(out=out, in_=in_, **kw))
            return None
        if eng == "gpsimd":
            self.wait(eng, sem, self.val(sem))
        v = self.bump(sem, 16)
        h = self.S(sem)
        self.op(eng, lambda E: E.dma_start(out=out, in_=in_, **kw).then_inc(h, 16))
        return v

    def wait(self, eng, sem, v):
        if v <= 0:
            return
        h = self.S(sem)
        self.op(eng, lambda E: E.wait_ge(h, v))

    def run(self, eng, fn, sem=None, k=1):
        if sem is None:
            self.op(eng, fn)
            return None
        v = self.bump(sem, k)
        h = self.S(sem)
        self.op(eng, lambda E: fn(E).then_inc(h, k))
        return v

    def seq(self, eng, fns, sem):
        h = self.S(sem)
        v = None
        for i, f in enumerate(fns):
            if v is not None:
                self.op(eng, lambda E, v=v: E.wait_ge(h, v))
            v = self.bump(sem, 1)
            self.op(eng, lambda E, f=f: f(E).then_inc(h, 1))
        return v

    def emit(self):
        for ph in self.phases:
            if not any(ph[e] for e in ENGINES):
                continue
            with self.nc.Block() as block:
                for e in ENGINES:
                    ops = ph[e]
                    if not ops:
                        continue
                    getattr(block, e)(lambda E, ops=ops: [f(E) for f in ops])


def sbt(stack, nc, name, shape, dtype):
    return stack.enter_context(nc.sbuf_tensor(name, shape, dtype))


def pst(stack, nc, name, shape, dtype):
    return stack.enter_context(nc.psum_tensor(name, shape, dtype))


def brow_dram(t_ap, offset, n, reps=128):
    return AP(t_ap.tensor, offset, [[0, reps], [1, n]])



def emit_ln(p, x_dram, xnfm, ident, B, tag):
    x2, xntok, sqscr, st, ps2 = B["x2"], B["xntok"], B["sqscr"], B["st"], B["ps2"]
    eps_col = B["eps_col"]
    ld, stg, tk, tr, ev = (tag + s for s in ("ld", "st", "tk", "tr", "ev"))
    sums = st[:, 0:1]
    sumsq = st[:, 1:2]
    m = st[:, 2:3]
    msq = st[:, 3:4]
    var = st[:, 4:5]
    lnv = st[:, 5:6]
    rstd = st[:, 6:7]
    for j in range(NT):
        b = 0
        if j >= 1:
            p.wait("sync", tk, j)
        vld = p.dma("sync", x2[b][:], x_dram[128 * j:128 * j + 128, :], sem=ld)
        p.wait("vector", ld, vld)
        vsu = p.run("vector", lambda v, b=b: v.reduce_sum(sums, x2[b][:], axis=AX.X),
                    sem=stg)
        p.wait("scalar", ld, vld)
        p.wait("scalar", stg, vsu)
        vsq = p.run("scalar", lambda sc, b=b: sc.activation(
            sqscr[:], x2[b][:], AF.Square, accum_out=sumsq), sem=stg)
        p.wait("vector", stg, vsq)

        vva = p.seq("vector", [
            lambda v: v.tensor_scalar(m, sums, 1.0 / D, None, ALU.mult),
            lambda v: v.tensor_scalar(var, sumsq, 1.0 / D, None, ALU.mult),
            lambda v: v.tensor_tensor(msq, m, m, ALU.mult),
            lambda v: v.tensor_tensor(var, var, msq, ALU.subtract),
        ], stg)
        p.wait("scalar", stg, vva)

        vrs = p.seq("scalar", [
            lambda sc: sc.activation(lnv, var, AF.Ln, bias=eps_col),
            lambda sc: sc.activation(rstd, lnv, AF.Exp, scale=-0.5),
        ], stg)
        p.wait("vector", stg, vrs)
        if j >= 1:
            p.wait("vector", tr, j * NKC)
        vtk = p.run("vector", lambda v, b=b: v.tensor_scalar(
            xntok[:], x2[b][:], m, rstd, ALU.subtract, ALU.mult), sem=tk)
        p.wait("tensor", tk, vtk)
        for c in range(NKC):
            if j * NKC + c >= 2:
                p.wait("tensor", ev, j * NKC + c - 1)
            vtr = p.run("tensor", lambda t, c=c: t.transpose(
                ps2[c % 2][:].bitcast(BF16)[:, 0:128],
                xntok[:, 128 * c:128 * c + 128], ident[:]), sem=tr)
            p.wait("vector", tr, vtr)
            p.run("vector", lambda v, c=c, j=j: v.tensor_copy(
                xnfm[:, T * c + 128 * j:T * c + 128 * j + 128],
                ps2[c % 2][:].bitcast(BF16)[:, 0:128]), sem=ev)



def build_launch_A(maxphase=99):
    nc = bass.Bass("TRN2", target_bir_lowering=False)
    dr = lambda n, shp, dt, kind="ExternalInput": nc.dram_tensor(n, shp, dt, kind=kind).ap()
    x_d = dr("x", [T, D], F32)
    wqkv_d = dr("wqkv", [D, 3072], F32)
    bqkv_d = dr("bqkv", [3072], F32)
    w2_d = dr("w2", [D, D], F32)
    b_d = dr("bffn", [D], F32)
    wqT_d = dr("wqT", [D, 256], F32)
    wkT_d = dr("wkT", [D, 256], F32)
    ars_d = dr("ars", [D, 128], F32)
    b2_d = dr("b2", [128, D], F32)
    ident_d = dr("ident", [128, 128], BF16)
    shm_d = dr("shm", [128, 2], U8)
    gB_d = dr("gB", [2048, 1376], U8)
    uB_d = dr("uB", [2048, 1376], U8)
    dB_d = dr("dB", [5504, 512], U8)
    gS_d = dr("gS", [32, 1376], F16)
    uS_d = dr("uS", [32, 1376], F16)
    dS_d = dr("dS", [86, 512], F16)
    gZ_d = dr("gZ", [32, 172], I32)
    uZ_d = dr("uZ", [32, 172], I32)
    dZ_d = dr("dZ", [86, 64], I32)
    zsh_d = dr("zsh", [128, 8], F32)
    zmk_d = dr("zmk", [128, 1], I32)

    q0_d = dr("q0", [1024, T], BF16, "ExternalOutput")
    k0_d = dr("k0", [T, 1024], BF16, "ExternalOutput")
    v0_d = dr("v0", [T, 1024], BF16, "ExternalOutput")
    r0p_d = dr("r0p", [T, D], F32, "ExternalOutput")
    wg_d = dr("wgsl", [D, 1376], BF16, "ExternalOutput")
    wu_d = dr("wusl", [D, 1376], BF16, "ExternalOutput")
    wd_d = dr("wdsl", [INTER, 512], BF16, "ExternalOutput")
    zpg_scr = dr("zpgscr", [32, 1376], F32, "Internal")
    zpu_scr = dr("zpuscr", [32, 1376], F32, "Internal")
    zpd_scr = dr("zpdscr", [86, 512], F32, "Internal")

    with ExitStack() as stack:
        p = Plan(nc, stack)
        x2 = [sbt(stack, nc, "x0sb", [128, D], F32)]
        xntok = sbt(stack, nc, "xntok", [128, D], BF16)
        sqscr = xntok
        st = sbt(stack, nc, "st", [128, 16], F32)
        xnfm = sbt(stack, nc, "xnfm", [128, NKC * T], BF16)
        ident = sbt(stack, nc, "identsb", [128, 128], BF16)
        wbf = [sbt(stack, nc, f"wbf{i}", [128, 4096], BF16) for i in range(2)]
        qstage = [sbt(stack, nc, f"qs{i}", [128, 512], BF16) for i in range(2)]
        biasq = sbt(stack, nc, "biasq", [128, 8], F32)
        biaskv = sbt(stack, nc, "biaskv", [128, 2048], F32)
        ostage = [sbt(stack, nc, f"os{i}", [128, 512], F32) for i in range(2)]
        xrl = [sbt(stack, nc, f"xrl{i}", [128, 512], F32) for i in range(2)]
        bfb = [sbt(stack, nc, f"bfb{i}", [128, 512], F32) for i in range(2)]
        wmix = sbt(stack, nc, "wmix", [128, NKC * 640], BF16)
        b2b = sbt(stack, nc, "b2b", [128, D], BF16)
        mixs = sbt(stack, nc, "mixs", [128, 1024], BF16)
        mixw = sbt(stack, nc, "mixw", [128, 64], F32)
        qk = sbt(stack, nc, "qk", [128, 256], F32)
        hwfm = sbt(stack, nc, "hwfm", [128, T], BF16)
        pp = [pst(stack, nc, f"pp{i}", [128, 1024], F32) for i in range(2)]
        ps_hi = [pst(stack, nc, f"ps{i}", [128, 512], F32) for i in range(4, 8)]
        ps = [pp[0][:, 0:512], pp[0][:, 512:1024],
              pp[1][:, 0:512], pp[1][:, 512:1024]] + ps_hi

        dq_t = [sbt(stack, nc, f"dqt{i}", [128, 512], F32) for i in range(2)]
        dq_s = sbt(stack, nc, "dqs", [128, 1376], F32)
        dq_zp = sbt(stack, nc, "dqzp", [128, 1376], F32)
        dq_zi = biaskv[:].bitcast(I32)
        dq_shm = sbt(stack, nc, "dqshm", [128, 2], U8)
        dq_zsh = sbt(stack, nc, "dqzsh", [128, 8], F32)
        dq_zmk = sbt(stack, nc, "dqzmk", [128, 1], I32)

        cst = sbt(stack, nc, "cstsb", [128, 4], F32)
        p.run("vector", lambda v: v.memset(cst[:, 0:1], EPS))
        p.dma("sync", ident[:], ident_d[:], sem="c0")
        for mm in range(8):
            p.dma("sync", biasq[:, mm:mm + 1],
                  AP(bqkv_d.tensor, 128 * mm, [[1, 128], [1, 1]]), sem="c0")
        p.dma("sync", biaskv[:], brow_dram(bqkv_d, 1024, 2048), sem="c0")
        p.dma("gpsimd", b2b[:], b2_d[:], sem="c0g")
        p.dma("gpsimd",
              AP(wmix.ap().tensor, 0, [[NKC * 640, 128], [640, NKC], [1, 256]]),
              AP(wqT_d.tensor, 0, [[256, 128], [128 * 256, NKC], [1, 256]]), sem="c0g")
        p.dma("gpsimd",
              AP(wmix.ap().tensor, 256, [[NKC * 640, 128], [640, NKC], [1, 256]]),
              AP(wkT_d.tensor, 0, [[256, 128], [128 * 256, NKC], [1, 256]]), sem="c0g")
        p.dma("gpsimd",
              AP(wmix.ap().tensor, 512, [[NKC * 640, 128], [640, NKC], [1, 128]]),
              AP(ars_d.tensor, 0, [[128, 128], [128 * 128, NKC], [1, 128]]), sem="c0g")
        p.dma("sync", dq_shm[:], shm_d[:], sem="c0")
        p.dma("sync", dq_zsh[:], zsh_d[:], sem="c0")
        p.dma("sync", dq_zmk[:], zmk_d[:], sem="c0")
        c0tot = p.val("c0")
        c0gtot = p.val("c0g")
        p.new_phase()
        for _e in ("tensor", "vector", "scalar"):
            p.wait(_e, "c0", c0tot)
            p.wait(_e, "c0g", c0gtot)

        if maxphase < 1:
            p.emit()
            return nc
        emit_ln(p, x_d, xnfm, ident,
                dict(x2=x2, xntok=xntok, sqscr=sqscr, st=st, ps2=ps[6:8],
                     eps_col=cst[:, 0:1]), "ln")
        p.new_phase()

        if maxphase < 2:
            p.emit()
            return nc
        for mm in range(8):
            b = mm % 2
            if mm >= 2:
                p.wait("gpsimd", "p2mm", 2 * mm - 2)
            vw = p.dma("gpsimd", wbf[b][:],
                       AP(wqkv_d.tensor, 128 * mm,
                          [[3072, 128], [128 * 3072, NKC], [1, 128]]), sem=f"p2w{b}")
            for tb in range(2):
                idx = 2 * mm + tb
                pj = ps[idx % 4]
                p.wait("tensor", f"p2w{b}", vw)
                if idx >= 4:
                    p.wait("tensor", "p2ev", idx - 3)

                def fmm(t, pj=pj, b=b, tb=tb):
                    for kc in range(NKC):
                        r = t.matmul(pj[:], wbf[b][:, 128 * kc:128 * kc + 128],
                                     xnfm[:, T * kc + 512 * tb:T * kc + 512 * tb + 512],
                                     start=(kc == 0), stop=(kc == NKC - 1))
                    return r
                vm = p.run("tensor", fmm, sem="p2mm")
                stg = qstage[idx % 2]
                p.wait("vector", "p2mm", vm)
                if idx >= 2:
                    p.wait("vector", f"p2st{idx % 2}", p.val(f"p2st{idx % 2}"))
                vev = p.run("vector", lambda v, pj=pj, stg=stg, mm=mm: v.tensor_scalar(
                    stg[:], pj[:], biasq[:, mm:mm + 1], None, ALU.add), sem="p2ev")
                p.wait("gpsimd", "p2ev", vev)
                p.dma("gpsimd", q0_d[128 * mm:128 * mm + 128, 512 * tb:512 * tb + 512],
                      stg[:], sem=f"p2st{idx % 2}")
        p.new_phase()

        if maxphase < 3:
            for s_ in ("p2st",):
                p.wait("gpsimd", s_, p.val(s_))
            p.emit()
            return nc
        for _b in range(2):
            p.wait("vector", f"p2st{_b}", p.val(f"p2st{_b}"))
        for cb in range(4):
            out_d = k0_d if cb < 2 else v0_d
            ocol = (cb % 2) * 512
            for kc in range(NKC):
                b = kc % 2
                gidx = cb * NKC + kc
                if gidx >= 2:
                    p.wait("gpsimd", "p3mm", gidx - 1)
                vw = p.dma("gpsimd", wbf[b][:, 0:512],
                           AP(wqkv_d.tensor, 3072 * 128 * kc + 1024 + 512 * cb,
                              [[3072, 128], [1, 512]]), sem=f"p3w{b}")
                p.wait("tensor", f"p3w{b}", vw)
                if kc == 0 and cb > 0:
                    p.wait("tensor", "p3ev", 8 * cb)

                def fmm3(t, b=b, kc=kc):
                    for tok in range(NT):
                        r = t.matmul(ps[tok][:],
                                     xnfm[:, T * kc + 128 * tok:T * kc + 128 * tok + 128],
                                     wbf[b][:, 0:512],
                                     start=(kc == 0), stop=(kc == NKC - 1))
                    return r
                p.run("tensor", fmm3, sem="p3mm")
            vmm = p.val("p3mm")
            for tok in range(NT):
                sidx = cb * NT + tok
                p.wait("vector", "p3mm", vmm)
                if sidx >= 2:
                    p.wait("vector", f"p3st{sidx % 2}", p.val(f"p3st{sidx % 2}"))
                stg = qstage[sidx % 2]
                vev = p.run("vector", lambda v, tok=tok, cb=cb, stg=stg: v.tensor_tensor(
                    stg[:], ps[tok][:],
                    biaskv[:, 512 * cb:512 * cb + 512], ALU.add), sem="p3ev")
                p.wait("gpsimd", "p3ev", vev)
                p.dma("gpsimd", out_d[128 * tok:128 * tok + 128, ocol:ocol + 512],
                      stg[:], sem=f"p3st{sidx % 2}")
        p.new_phase()

        if maxphase < 4:
            for s_ in ("p2st", "p3st"):
                p.wait("gpsimd", s_, p.val(s_))
            p.emit()
            return nc
        qf = mixs[:, 0:256]
        kf = mixs[:, 256:512]
        hf3 = AP(mixs.ap().tensor, 512, [[1024, 128], [16, 8], [1, 16]])
        hw3 = AP(mixs.ap().tensor, 640, [[1024, 128], [16, 8], [1, 16]])
        hw2 = mixs[:, 640:768]
        scv = mixw[:, 0:8]
        negmax = mixw[:, 8:9]
        negmax_s = mixw[:, 9:10]
        expw = mixw[:, 16:24]
        sumw = mixw[:, 24:25]
        rzw = mixw[:, 25:26]
        wv = mixw[:, 32:40]
        wv3 = AP(mixw.ap().tensor, 32, [[64, 128], [1, 8], [0, 16]])
        for tok in range(NT):
            if tok >= 1:
                p.wait("tensor", "p4tr", tok)

            def fmm4(t, tok=tok):
                for kc in range(NKC):
                    xs = xnfm[:, T * kc + 128 * tok:T * kc + 128 * tok + 128]
                    t.matmul(ps[0][:, 0:256], xs, wmix[:, 640 * kc:640 * kc + 256],
                             start=(kc == 0), stop=(kc == NKC - 1))
                    t.matmul(ps[1][:, 0:256], xs, wmix[:, 640 * kc + 256:640 * kc + 512],
                             start=(kc == 0), stop=(kc == NKC - 1))
                    r = t.matmul(ps[2][:, 0:128], xs,
                                 wmix[:, 640 * kc + 512:640 * kc + 640],
                                 start=(kc == 0), stop=(kc == NKC - 1))
                return r
            vm = p.run("tensor", fmm4, sem="p4mm")
            p.wait("vector", "p4mm", vm)

            vv = p.seq("vector", [
                lambda v: v.tensor_copy(qf, ps[0][:, 0:256]),
                lambda v: v.tensor_copy(kf, ps[1][:, 0:256]),
                lambda v: v.tensor_copy(mixs[:, 512:640], ps[2][:, 0:128]),
                lambda v: v.tensor_tensor(qk[:], qf, kf, ALU.mult),
                lambda v: v.reduce_sum(scv, AP(qk.ap().tensor, 0,
                                               [[256, 128], [32, 8], [1, 32]]),
                                       axis=AX.X),
                lambda v: v.reduce_max(negmax, scv, axis=AX.X, negate=True),
                lambda v: v.tensor_scalar(negmax_s, negmax, 1.0 / math.sqrt(32.0),
                                          None, ALU.mult),
            ], "p4v")
            p.wait("scalar", "p4v", vv)
            vsc = p.run("scalar", lambda sc: sc.activation(
                expw, scv, AF.Exp, bias=negmax_s, scale=1.0 / math.sqrt(32.0),
                accum_out=sumw), sem="p4s")
            p.wait("vector", "p4s", vsc)

            vw4 = p.seq("vector", [
                lambda v: v.reciprocal(rzw, sumw),
                lambda v: v.tensor_scalar(wv, expw, rzw, None, ALU.mult),
                lambda v: v.tensor_tensor(hw3, hf3, wv3, ALU.mult),
            ], "p4w")
            p.wait("tensor", "p4w", vw4)
            vtr = p.run("tensor", lambda t: t.transpose(
                ps[4][:].bitcast(BF16)[:, 0:128], hw2, ident[:]), sem="p4tr0")
            p.wait("vector", "p4tr0", vtr)
            p.run("vector", lambda v, tok=tok: v.tensor_copy(
                hwfm[:, 128 * tok:128 * tok + 128], ps[4][:].bitcast(BF16)[:, 0:128]),
                sem="p4tr")
        p.new_phase()

        if maxphase < 5:
            for s_ in ("p2st", "p3st"):
                p.wait("gpsimd", s_, p.val(s_))
            p.emit()
            return nc
        def dq_prep(tag, S_d, Z_d, ZP_d, G, N, prev=None):
            if prev is not None:
                for e in ("sync", "gpsimd"):
                    p.wait(e, prev + "zp", p.val(prev + "zp"))
                    p.wait(e, prev + "zw", p.val(prev + "zw"))
            vs = p.dma("gpsimd", dq_s[0:G, 0:N], S_d[:], sem=tag + "s")
            vz = p.dma("sync", dq_zi[0:G, 0:N // 8], Z_d[:], sem=tag + "z")
            p.wait("vector", tag + "s", vs)
            p.wait("vector", tag + "z", vz)

            def fz_unpack(v, G=G, N=N):
                for i in range(8):
                    r = v.tensor_scalar(
                        AP(dq_zi.tensor, 176 + i, [[2048, G], [8, N // 8]]),
                        dq_zi[0:G, 0:N // 8],
                        4 * i, 15,
                        ALU.logical_shift_right, ALU.bitwise_and)
                return r
            vzp = p.seq("vector", [
                fz_unpack,
                lambda v, G=G, N=N: v.scalar_tensor_tensor(
                    dq_zp[0:G, 0:N],
                    AP(dq_zi.tensor, 176, [[2048, G], [1, N]]),
                    1.0, dq_s[0:G, 0:N], ALU.add, ALU.mult),
            ], tag + "zp")
            p.wait("gpsimd", tag + "zp", vzp)
            vzw = p.dma("gpsimd", ZP_d[:], dq_zp[0:G, 0:N], sem=tag + "zw")
            p.wait("scalar", tag + "zw", vzw)
            p.wait("gpsimd", tag + "zw", vzw)

        wmix_i32 = wmix[:].bitcast(I32)
        wmix_u8 = wmix[:].bitcast(U8)
        wmix_f32 = wmix[:].bitcast(F32)
        wmix_bf = wmix
        dq_qa = [wmix_i32[:, 0:512], wmix_i32[:, 512:1024]]
        dq_qb = [wmix_i32[:, 1024:1536], wmix_i32[:, 1536:2048]]
        dq_u4 = [wmix_u8[:, 8192 + 512 * k:8192 + 512 * k + 512] for k in range(4)]
        dq_sr4 = [wmix_f32[:, 2560 + 512 * k:2560 + 512 * k + 512] for k in range(4)]
        dq_zr4 = [wmix_f32[:, 4608 + 512 * k:4608 + 512 * k + 512] for k in range(4)]
        dq_w4 = [wmix_bf[:, 13312 + 512 * k:13312 + 512 * k + 512] for k in range(4)]

        def dq_tile_emitters():
            tiles = []
            state = {"ti": 0, "pending": []}
            gu_blocks = [(0, 512), (512, 512), (1024, 352)]
            for tag, B_d, S_d, ZP_d, W_d, G, N, nblocks in (
                    ("dg", gB_d, gS_d, zpg_scr, wg_d, 32, 1376, gu_blocks),
                    ("du", uB_d, uS_d, zpu_scr, wu_d, 32, 1376, gu_blocks),
                    ("dd", dB_d, dS_d, zpd_scr, wd_d, 86, 512, [(0, 512)])):
                for kc in range(G):
                    for (nb0, nbw) in nblocks:
                        def emit(tag=tag, B_d=B_d, S_d=S_d, ZP_d=ZP_d, W_d=W_d,
                                 G=G, N=N, kc=kc, nb0=nb0, nbw=nbw):
                            ti = state["ti"]
                            state["ti"] += 1
                            b = ti % 2
                            b4 = ti % 4
                            if ti >= 4:
                                p.wait("sync", "dqq", 5 * (ti - 3))
                                p.wait("scalar", "dqq", 5 * (ti - 3))
                                p.wait("gpsimd", "dqq", 5 * (ti - 3))
                            vu = p.dma("sync", dq_u4[b4][:, 0:nbw],
                                       AP(B_d.tensor, (64 * kc) * N + nb0,
                                          [[0, 2], [N, 64], [1, nbw]]),
                                       sem=f"dqu{b4}")
                            vsr = p.dma("gpsimd", dq_sr4[b4][:, 0:nbw],
                                        brow_dram(S_d, kc * N + nb0, nbw),
                                        sem=f"dqsr{b4}")
                            vzr = p.dma("scalar", dq_zr4[b4][:, 0:nbw],
                                        brow_dram(ZP_d, kc * N + nb0, nbw),
                                        sem=f"dqzr{b4}")
                            p.wait("vector", f"dqu{b4}", vu)
                            p.wait("vector", f"dqsr{b4}", vsr)
                            p.wait("vector", f"dqzr{b4}", vzr)
                            if ti >= 4:
                                p.wait("vector", f"dqws{ti % 4}",
                                       p.val(f"dqws{ti % 4}"))
                            vq = p.seq("vector", [
                                lambda v, b=b, b4=b4, nbw=nbw: v.tensor_copy(
                                    dq_qa[b][:, 0:nbw], dq_u4[b4][:, 0:nbw]),
                                lambda v, b=b, nbw=nbw: v.tensor_single_scalar(
                                    dq_qb[b][0:64, 0:nbw], dq_qa[b][0:64, 0:nbw],
                                    15, ALU.bitwise_and),
                                lambda v, b=b, nbw=nbw: v.tensor_scalar(
                                    dq_qb[b][64:128, 0:nbw], dq_qa[b][64:128, 0:nbw],
                                    4, 15, ALU.logical_shift_right, ALU.bitwise_and),
                                lambda v, b=b, b4=b4, nbw=nbw: v.tensor_tensor(
                                    dq_t[b][:, 0:nbw], dq_qb[b][:, 0:nbw],
                                    dq_sr4[b4][:, 0:nbw], ALU.mult),
                                lambda v, b=b, b4=b4, nbw=nbw: v.tensor_tensor(
                                    dq_w4[b4][:, 0:nbw], dq_t[b][:, 0:nbw],
                                    dq_zr4[b4][:, 0:nbw], ALU.subtract),
                            ], "dqq")
                            def do_store(W_d=W_d, kc=kc, N=N, nb0=nb0, nbw=nbw,
                                         b4=b4, ti=ti, vq=vq):
                                p.wait("scalar", "dqq", vq)
                                p.dma("scalar",
                                      AP(W_d.tensor, (128 * kc) * N + nb0,
                                         [[2 * N, 64], [1, nbw]]),
                                      dq_w4[b4][0:64, 0:nbw], sem=f"dqws{ti % 4}")
                                p.dma("scalar",
                                      AP(W_d.tensor, (128 * kc + 1) * N + nb0,
                                         [[2 * N, 64], [1, nbw]]),
                                      dq_w4[b4][64:128, 0:nbw], sem=f"dqws{ti % 4}")
                            state["pending"].append(do_store)
                            if len(state["pending"]) > 2:
                                state["pending"].pop(0)()
                        tiles.append(emit)
            return tiles, state["pending"]

        wf32 = [x2[0][:, 0:512], x2[0][:, 512:1024]]
        dq_prep("dg", gS_d, gZ_d, zpg_scr, 32, 1376)
        dq_prep("du", uS_d, uZ_d, zpu_scr, 32, 1376, prev="dg")
        dq_prep("dd", dS_d, dZ_d, zpd_scr, 86, 512, prev="du")
        dq_tiles, dq_state_pending = dq_tile_emitters()
        ndq = len(dq_tiles)
        dqi = 0
        for ob in range(8):
            bb = bfb[ob % 2]
            if ob >= 2:
                p.wait("sync", "p5ev", 16 * (ob - 1))
            vb = p.dma("sync", bb[:], brow_dram(b_d, 512 * ob, 512), sem=f"p5b{ob % 2}")
            for kc in range(NKC):
                b = kc % 2
                gidx = ob * (NKC + 1) + kc
                if gidx >= 2:
                    p.wait("sync", "p5mm", p.val("p5mm") - 1)
                vwr = p.dma("sync", wf32[b],
                            AP(w2_d.tensor, D * 128 * kc + 512 * ob,
                               [[D, 128], [1, 512]]), sem=f"p5wr{b}")
                p.wait("scalar", f"p5wr{b}", vwr)
                if gidx >= 2:
                    p.wait("scalar", "p5mm", p.val("p5mm") - 1)
                vw = p.run("scalar", lambda sc, b=b: sc.activation(
                    wbf[b][:, 0:512], wf32[b], AF.Copy), sem="p5wc")
                p.wait("tensor", "p5wc", vw)
                if kc == 0 and ob > 0:
                    p.wait("tensor", "p5ev", 16 * ob)

                def fmm5(t, b=b, kc=kc):
                    for tok in range(NT):
                        r = t.matmul(ps[tok][:],
                                     xnfm[:, T * kc + 128 * tok:T * kc + 128 * tok + 128],
                                     wbf[b][:, 0:512], start=(kc == 0), stop=False)
                    return r
                p.run("tensor", fmm5, sem="p5mm")
                take = (ndq * (ob * NKC + kc + 1)) // (8 * NKC) - dqi
                for _ in range(take):
                    dq_tiles[dqi]()
                    dqi += 1

            def fmm5b(t, ob=ob):
                for tok in range(NT):
                    r = t.matmul(ps[tok][:], hwfm[:, 128 * tok:128 * tok + 128],
                                 b2b[:, 512 * ob:512 * ob + 512], start=False, stop=True)
                return r
            vm5 = p.run("tensor", fmm5b, sem="p5mm")
            for tok in range(NT):
                sidx = ob * NT + tok
                bx = xrl[tok % 2]
                if sidx >= 2:
                    p.wait("sync", "p5ev", 2 * (sidx - 1))
                vx = p.dma("sync", bx[:],
                           x_d[128 * tok:128 * tok + 128, 512 * ob:512 * ob + 512],
                           sem=f"p5x{tok % 2}")
                os_b = ostage[tok % 2]
                p.wait("vector", "p5mm", vm5)
                p.wait("vector", f"p5b{ob % 2}", vb)
                p.wait("vector", f"p5x{tok % 2}", vx)
                if sidx >= 2:
                    p.wait("vector", f"p5st{sidx % 2}", p.val(f"p5st{sidx % 2}"))

                vev = p.seq("vector", [
                    lambda v, tok=tok, bb=bb, os_b=os_b:
                        v.tensor_tensor(os_b[:], ps[tok][:], bb[:], ALU.add),
                    lambda v, bx=bx, os_b=os_b:
                        v.tensor_tensor(os_b[:], os_b[:], bx[:], ALU.add),
                ], "p5ev")
                p.wait("gpsimd", "p5ev", vev)
                p.dma("gpsimd", r0p_d[128 * tok:128 * tok + 128, 512 * ob:512 * ob + 512],
                      os_b[:], sem=f"p5st{sidx % 2}")
        p.new_phase()

        p.new_phase()
        for fn_ in dq_state_pending:
            fn_()
        for sem_name in ("p2st0", "p2st1", "p3st0", "p3st1", "p5st0", "p5st1"):
            p.wait("gpsimd", sem_name, p.val(sem_name))
        for sem_name in ("dqws0", "dqws1", "dqws2", "dqws3"):
            p.wait("scalar", sem_name, p.val(sem_name))
        p.emit()
    return nc


def prep_launch_A(inputs):
    x = np.ascontiguousarray(inputs["L0_attn_x"][0])
    xs = stripe(x)
    w2 = np.ascontiguousarray(inputs["L0_ffn_W"].T)
    wqT = np.ascontiguousarray(inputs["L0_ffn_Wq"].T)
    wkT = np.ascontiguousarray(inputs["L0_ffn_Wk"].T)
    ars = np.ascontiguousarray(inputs["L0_ffn_A"].reshape(128, D).T)
    b2 = np.ascontiguousarray(inputs["L0_ffn_Bm"].transpose(0, 2, 1).reshape(128, D))
    ident = np.eye(128, dtype=bf16)
    shm = np.stack([(4 * (np.arange(128) % 2)).astype(np.uint8),
                    np.full(128, 15, np.uint8)], axis=1)
    zsh = np.zeros((128, 8), np.float32)
    zmk = np.full((128, 1), 15, np.int32)
    gB = pack_bytes(inputs["L1_ffn_gate_qweight"])
    uB = pack_bytes(inputs["L1_ffn_up_qweight"])
    dB = pack_bytes(inputs["L1_ffn_down_qweight"])
    maps = []
    for r in range(NCORES):
        n0, n1 = 1376 * r, 1376 * (r + 1)
        c0, c1 = 512 * r, 512 * (r + 1)
        maps.append({
            "x": xs[r],
            "wqkv": inputs["L0_attn_Wqkv"], "bqkv": inputs["L0_attn_bqkv"],
            "w2": w2, "bffn": inputs["L0_ffn_b"],
            "wqT": wqT, "wkT": wkT, "ars": ars, "b2": b2,
            "ident": ident, "shm": shm, "zsh": zsh, "zmk": zmk,
            "gB": np.ascontiguousarray(gB[:, n0:n1]),
            "uB": np.ascontiguousarray(uB[:, n0:n1]),
            "dB": np.ascontiguousarray(dB[:, c0:c1]),
            "gS": np.ascontiguousarray(inputs["L1_ffn_gate_scales"][:, n0:n1]),
            "uS": np.ascontiguousarray(inputs["L1_ffn_up_scales"][:, n0:n1]),
            "dS": np.ascontiguousarray(inputs["L1_ffn_down_scales"][:, c0:c1]),
            "gZ": np.ascontiguousarray(inputs["L1_ffn_gate_zeros"][:, 172 * r:172 * (r + 1)]),
            "uZ": np.ascontiguousarray(inputs["L1_ffn_up_zeros"][:, 172 * r:172 * (r + 1)]),
            "dZ": np.ascontiguousarray(inputs["L1_ffn_down_zeros"][:, 64 * r:64 * (r + 1)]),
        })
    return maps





def build_launch_B():
    nc = bass.Bass("TRN2", target_bir_lowering=False)
    dr = lambda n, shp, dt, kind="ExternalInput": nc.dram_tensor(n, shp, dt, kind=kind).ap()
    q0_d = dr("q0", [1024, T], BF16)
    k0f_d = dr("k0fT", [1024, S], BF16)
    v0f_d = dr("v0f", [S, 1024], BF16)
    r0p_d = dr("r0p", [T, D], F32)
    wo_d = dr("wo", [1024, D], F32)
    bo_d = dr("bo", [D], F32)
    wqkv1_d = dr("wqkv1", [D, 4352], F32)
    ident_d = dr("ident", [128, 128], BF16)
    x1_d = dr("x1", [T, D], F32, "ExternalOutput")
    xn1_d = dr("xn1", [D, T], BF16, "ExternalOutput")
    q1_d = dr("q1", [D, T], BF16, "ExternalOutput")
    k1_d = dr("k1", [T, 128], BF16, "ExternalOutput")
    v1_d = dr("v1", [T, 128], BF16, "ExternalOutput")

    with ExitStack() as stack:
        p = Plan(nc, stack)
        big = sbt(stack, nc, "big", [128, NKC * T], BF16)
        q0sb = sbt(stack, nc, "q0sb", [128, 8 * T], BF16)
        ofm = sbt(stack, nc, "ofm", [128, D], BF16)
        kfmt = [sbt(stack, nc, f"kfmt{i}", [128, 1024], BF16) for i in range(2)]
        vt = [sbt(stack, nc, f"vt{i}", [128, 1024], BF16) for i in range(2)]
        wbf = [sbt(stack, nc, f"wbf{i}", [128, 4096], BF16) for i in range(2)]
        stg4 = [sbt(stack, nc, f"stg{i}", [128, 512], F32) for i in range(2)]
        bstg = [sbt(stack, nc, f"bstg{i}", [128, 512], BF16) for i in range(2)]
        bob = [sbt(stack, nc, f"bob{i}", [128, 512], F32) for i in range(2)]
        rpl = [sbt(stack, nc, f"rpl{i}", [128, 512], F32) for i in range(2)]
        zsb = sbt(stack, nc, "zsb", [128, 16], F32)
        zrow = sbt(stack, nc, "zrow", [128, 512], F32)
        cst = sbt(stack, nc, "cstsb", [128, 4], F32)
        ones_col = sbt(stack, nc, "onescol", [128, 1], BF16)
        ident = sbt(stack, nc, "identsb", [128, 128], BF16)
        x2 = [sbt(stack, nc, "x2sb", [128, D], F32)]
        st = sbt(stack, nc, "stsb", [128, 16], F32)
        pp = [pst(stack, nc, f"pp{i}", [128, 1024], F32) for i in range(2)]
        ps_hi = [pst(stack, nc, f"ps{i}", [128, 512], F32) for i in range(4, 8)]
        ps = [pp[0][:, 0:512], pp[0][:, 512:1024],
              pp[1][:, 0:512], pp[1][:, 512:1024]] + ps_hi

        p.run("vector", lambda v: v.memset(cst[:, 0:1], EPS))
        p.run("vector", lambda v: v.memset(cst[:, 1:2], -M0))
        p.run("vector", lambda v: v.memset(ones_col[:], 1.0))
        p.dma("sync", ident[:], ident_d[:], sem="c0")
        p.dma("sync", q0sb[:],
              AP(q0_d.tensor, 0, [[1024, 128], [128 * 1024, 8], [1, 1024]]), sem="c0")
        c0tot = p.val("c0")
        p.new_phase()
        for _e in ("tensor", "vector", "scalar"):
            p.wait(_e, "c0", c0tot)

        for tb in range(2):
            for tk in range(64):
                kb = tk % 2
                if tk >= 2:
                    p.wait("sync", "b1mm", p.val("b1mm") - 1)
                vk = p.dma("sync", kfmt[kb][:],
                           AP(k0f_d.tensor, 128 * tk,
                              [[S, 128], [128 * S, 8], [1, 128]]),
                           sem=f"b1k{kb}")
                stps = ps[tk % 2]
                p.wait("tensor", f"b1k{kb}", vk)
                if tk >= 2:
                    p.wait("tensor", "b1x", p.val("b1x") - 1)

                def fst(t, stps=stps, kb=kb, tb=tb):
                    for dc in range(8):
                        r = t.matmul(stps[:], kfmt[kb][:, 128 * dc:128 * dc + 128],
                                     q0sb[:, 1024 * dc + 512 * tb:1024 * dc + 512 * tb + 512],
                                     start=(dc == 0), stop=(dc == 7))
                    return r
                vm = p.run("tensor", fst, sem="b1mm")
                p.wait("scalar", "b1mm", vm)
                vx = p.run("scalar", lambda sc, stps=stps, tk=tk: sc.activation(
                    big[:, 512 * tk:512 * tk + 512], stps[:], AF.Exp,
                    bias=cst[:, 1:2], scale=1.0 / 32.0), sem="b1x")
                p.wait("tensor", "b1x", vx)
                p.run("tensor", lambda t, tk=tk: t.matmul(
                    ps[2][0:1, :], ones_col[:], big[:, 512 * tk:512 * tk + 512],
                    start=(tk == 0), stop=(tk == 63)), sem="b1z")
            vz = p.val("b1z")
            p.wait("vector", "b1z", vz)
            vzr = p.seq("vector", [
                lambda v: v.tensor_copy(zrow[0:1, :], ps[2][0:1, :]),
                lambda v: v.reciprocal(zrow[32:33, :], zrow[0:1, :]),
            ], "b1zr")
            for tc in range(4):
                p.wait("sync", "b1zr", vzr)
                p.dma("sync", zsb[0:128, 8 + tc:9 + tc],
                      AP(zrow.ap().tensor, 32 * 512 + 128 * tc, [[512, 1], [1, 128]]),
                      sem="b1zc")
            vzc = p.val("b1zc")
            p.wait("vector", "b1zc", vzc)
            p.new_phase()
            if tb > 0:
                p.wait("tensor", "b2ev", p.val("b2ev"))
            for tk in range(64):
                vb = tk % 2
                if tk >= 2:
                    p.wait("sync", "b2mm", p.val("b2mm") - 1)
                vv = p.dma("sync", vt[vb][:],
                           v0f_d[128 * tk:128 * tk + 128, :], sem=f"b2v{vb}")
                p.wait("tensor", f"b2v{vb}", vv)

                def fpv(t, vb=vb, tk=tk):
                    for dc in range(8):
                        r = t.matmul(ps[dc][:], vt[vb][:, 128 * dc:128 * dc + 128],
                                     big[:, 512 * tk:512 * tk + 512],
                                     start=(tk == 0), stop=(tk == 63))
                    return r
                p.run("tensor", fpv, sem="b2mm")
            vm2 = p.val("b2mm")
            p.wait("vector", "b2mm", vm2)
            for dc in range(8):
                p.run("vector", lambda v, dc=dc: v.tensor_copy(
                    ofm[:, 512 * dc:512 * dc + 512], ps[dc][:]), sem="b2ev")
            p.new_phase()
            for mb in range(8):
                bb = bob[mb % 2]
                vbo = p.dma("sync", bb[:], brow_dram(bo_d, 512 * mb, 512),
                            sem=f"b3b{mb % 2}")
                for dc in range(8):
                    wb = dc % 2
                    gidx = mb * 8 + dc
                    if gidx >= 2:
                        p.wait("gpsimd", "b3mm", p.val("b3mm") - 1)
                    vw = p.dma("gpsimd", wbf[wb][:, 0:512],
                               AP(wo_d.tensor, D * 128 * dc + 512 * mb,
                                  [[D, 128], [1, 512]]), sem=f"b3w{wb}")
                    p.wait("tensor", f"b3w{wb}", vw)
                    if dc == 0 and (mb > 0 or tb > 0):
                        p.wait("tensor", "b3ev", p.val("b3ev"))

                    def fwo(t, wb=wb, dc=dc):
                        for tc in range(4):
                            r = t.matmul(ps[4 + tc][:],
                                         ofm[:, 512 * dc + 128 * tc:512 * dc + 128 * tc + 128],
                                         wbf[wb][:, 0:512], start=(dc == 0), stop=(dc == 7))
                        return r
                    p.run("tensor", fwo, sem="b3mm")
                vm3 = p.val("b3mm")
                for tc in range(4):
                    sidx = (tb * 8 + mb) * 4 + tc
                    tokj = tb * 4 + tc
                    if sidx >= 2:
                        p.wait("sync", "b3ev", p.val("b3ev") - 3)
                    vr = p.dma("sync", rpl[tc % 2][:],
                               r0p_d[128 * tokj:128 * tokj + 128, 512 * mb:512 * mb + 512],
                               sem=f"b3r{tc % 2}")
                    p.wait("vector", "b3mm", vm3)
                    p.wait("vector", f"b3b{mb % 2}", vbo)
                    p.wait("vector", f"b3r{tc % 2}", vr)
                    if sidx >= 2:
                        p.wait("vector", f"b3st{sidx % 2}", p.val(f"b3st{sidx % 2}"))
                    stg = stg4[tc % 2]
                    vev = p.seq("vector", [
                        lambda v, tc=tc, stg=stg: v.tensor_scalar(
                            stg[:], ps[4 + tc][:], zsb[:, 8 + tc:9 + tc], None, ALU.mult),
                        lambda v, stg=stg, bb=bb: v.tensor_tensor(
                            stg[:], stg[:], bb[:], ALU.add),
                        lambda v, stg=stg, tc=tc: v.tensor_tensor(
                            stg[:], stg[:], rpl[tc % 2][:], ALU.add),
                    ], "b3ev")
                    p.wait("gpsimd", "b3ev", vev)
                    p.dma("gpsimd", x1_d[128 * tokj:128 * tokj + 128,
                                         512 * mb:512 * mb + 512],
                          stg[:], sem=f"b3st{sidx % 2}")
            p.new_phase()

        for _b in range(2):
            p.wait("sync", f"b3st{_b}", p.val(f"b3st{_b}"))
        emit_ln(p, x1_d, big, ident,
                dict(x2=x2, xntok=ofm, sqscr=ofm, st=st, ps2=ps[6:8],
                     eps_col=cst[:, 0:1]), "lb")
        for c in range(NKC):
            p.wait("sync", "lbev", p.val("lbev"))
            p.dma("sync", xn1_d[128 * c:128 * c + 128, :],
                  big[:, T * c:T * c + T], sem="bxn")
        p.new_phase()

        for mc in range(NKC):
            b = mc % 2
            if mc >= 2:
                p.wait("gpsimd", "b5mm", 2 * mc - 2)
            vw = p.dma("gpsimd", wbf[b][:],
                       AP(wqkv1_d.tensor, 128 * mc,
                          [[4352, 128], [128 * 4352, NKC], [1, 128]]), sem=f"b5w{b}")
            for tb in range(2):
                idx = 2 * mc + tb
                pj = ps[idx % 4]
                p.wait("tensor", f"b5w{b}", vw)
                if idx >= 4:
                    p.wait("tensor", "b5ev", idx - 3)

                def fq1(t, pj=pj, b=b, tb=tb):
                    for kc in range(NKC):
                        r = t.matmul(pj[:], wbf[b][:, 128 * kc:128 * kc + 128],
                                     big[:, T * kc + 512 * tb:T * kc + 512 * tb + 512],
                                     start=(kc == 0), stop=(kc == NKC - 1))
                    return r
                vm = p.run("tensor", fq1, sem="b5mm")
                stg = bstg[idx % 2]
                p.wait("vector", "b5mm", vm)
                if idx >= 2:
                    p.wait("vector", f"b5st{idx % 2}", p.val(f"b5st{idx % 2}"))
                vev = p.run("vector", lambda v, pj=pj, stg=stg: v.tensor_copy(
                    stg[:], pj[:]), sem="b5ev")
                p.wait("gpsimd", "b5ev", vev)
                p.dma("gpsimd", q1_d[128 * mc:128 * mc + 128, 512 * tb:512 * tb + 512],
                      stg[:], sem=f"b5st{idx % 2}")
        for kc in range(NKC):
            b = kc % 2
            if kc >= 2:
                p.wait("gpsimd", "b6mm", kc - 1)
            vw = p.dma("gpsimd", wbf[b][:, 0:256],
                       AP(wqkv1_d.tensor, 4352 * 128 * kc + 4096,
                          [[4352, 128], [1, 256]]), sem=f"b6w{b}")
            p.wait("tensor", f"b6w{b}", vw)
            if kc == 0:
                p.wait("tensor", "b5ev", p.val("b5ev"))

            def fkv(t, b=b, kc=kc):
                for tok in range(NT):
                    r = t.matmul(ps[tok][:, 0:256],
                                 big[:, T * kc + 128 * tok:T * kc + 128 * tok + 128],
                                 wbf[b][:, 0:256], start=(kc == 0), stop=(kc == NKC - 1))
                return r
            p.run("tensor", fkv, sem="b6mm")
        vm6 = p.val("b6mm")
        for tok in range(NT):
            p.wait("vector", "b6mm", vm6)
            if tok >= 2:
                p.wait("vector", f"b6st{tok % 2}", p.val(f"b6st{tok % 2}"))
            stg = bstg[tok % 2]
            vev = p.run("vector", lambda v, tok=tok, stg=stg: v.tensor_copy(
                stg[:, 0:256], ps[tok][:, 0:256]), sem="b6ev")
            p.wait("gpsimd", "b6ev", vev)
            p.dma("gpsimd", k1_d[128 * tok:128 * tok + 128, :],
                  stg[:, 0:128], sem=f"b6st{tok % 2}")
            p.dma("gpsimd", v1_d[128 * tok:128 * tok + 128, :],
                  stg[:, 128:256], sem=f"b6st{tok % 2}")

        p.new_phase()
        for sem_name in ("b3st0", "b3st1", "bxn", "b5st0", "b5st1", "b6st0", "b6st1"):
            p.wait("gpsimd" if sem_name != "bxn" else "sync", sem_name, p.val(sem_name))
        p.emit()
    return nc


def prep_launch_B(inputs, a_results):
    k0f = np.ascontiguousarray(unstripe([np.asarray(r["k0"]) for r in a_results]).T)
    v0f = unstripe([np.asarray(r["v0"]) for r in a_results])
    ident = np.eye(128, dtype=bf16)
    maps = []
    for r in range(NCORES):
        maps.append({
            "q0": a_results[r]["q0"],
            "k0fT": k0f, "v0f": v0f,
            "r0p": a_results[r]["r0p"],
            "wo": inputs["L0_attn_Wo"], "bo": inputs["L0_attn_bo"],
            "wqkv1": inputs["L1_attn_w_qkv"],
            "ident": ident,
        })
    return maps



def attn1_spans():
    NW = []
    for h in range(32):
        Wh = math.ceil(WINC / float(SLOPES[h]))
        NW.append(math.ceil((Wh + 1151) / 512.0) + 1)
    out = {}
    for j in range(NT):
        for hg in range(8):
            cnt = min(2 * j + 2, NW[4 * hg + 3])
            u_hi = 2 * j + 1
            out[(j, hg)] = list(range(u_hi - cnt + 1, u_hi + 1))
    return out


def build_launch_C():
    nc = bass.Bass("TRN2", target_bir_lowering=False)
    dr = lambda n, shp, dt, kind="ExternalInput": nc.dram_tensor(n, shp, dt, kind=kind).ap()
    q1_d = dr("q1", [D, T], BF16)
    k1f_d = dr("k1fT", [128, S], BF16)
    v1f_d = dr("v1f", [S, 128], BF16)
    x1_d = dr("x1", [T, D], F32)
    xn1_d = dr("xn1", [D, T], BF16)
    wg_d = dr("wg", [D, INTER], BF16)
    wu_d = dr("wu", [D, INTER], BF16)
    wd_d = dr("wd", [INTER, D], BF16)
    wc_d = dr("wc", [D, D], F32)
    ramp_d = dr("ramp", [1, 512], F32)
    r2_d = dr("r2rows", [2, S], BF16)
    base_d = dr("basecols", [128, 256], F32)
    maskT_d = dr("maskT", [128, 1024], F32)
    out_d = dr("outp", [T, D], F32, "ExternalOutput")
    inter_scr = dr("interscr", [INTER, T], BF16, "Internal")
    x2p_scr = dr("x2pscr", [T, D], F32, "Internal")
    down_scr = dr("downscr", [D, T], F16, "Internal")

    spans = attn1_spans()
    SC1 = float(SCALE1)

    with ExitStack() as stack:
        p = Plan(nc, stack)
        mega = sbt(stack, nc, "mega", [128, 45056], BF16)
        k1fm = sbt(stack, nc, "k1fm", [128, S], BF16)
        v1sb = sbt(stack, nc, "v1sb", [128, S], BF16)
        q1g = sbt(stack, nc, "q1g", [128, 4 * T], BF16)
        qsc = sbt(stack, nc, "qsc", [128, 4 * T], BF16)
        maskT = sbt(stack, nc, "maskTsb", [128, 1024], F32)
        basec = sbt(stack, nc, "basec", [128, 256], F32)
        rowcol = sbt(stack, nc, "rowcol", [128, 8], F32)
        Psb = [sbt(stack, nc, f"Psb{i}", [128, 1024], BF16) for i in range(2)]
        PTbig = sbt(stack, nc, "PTbig", [128, 4096], BF16)
        ones2 = sbt(stack, nc, "ones2", [128, 512], BF16)
        R2sb = mega[:, 32768:32768 + S]
        PTbigs = [PTbig, mega[:, 40960:45056]]
        zparts = sbt(stack, nc, "zparts", [128, 96], F32)
        zrow = sbt(stack, nc, "zrowc", [128, 512], F32)
        zrec = sbt(stack, nc, "zrecc", [128, 512], F32)
        rzb = sbt(stack, nc, "rzb", [128, 512], F32)
        ones_row = sbt(stack, nc, "onesrow", [128, 128], F32)
        ident = sbt(stack, nc, "identsb", [128, 128], BF16)
        wsb = [sbt(stack, nc, f"wsb{i}", [128, 4096], BF16) for i in range(4)]
        stg = [sbt(stack, nc, f"stgc{i}", [128, 512], F32) for i in range(2)]
        bstg = [sbt(stack, nc, f"bstgc{i}", [128, 1024], BF16) for i in range(2)]
        fstg = [sbt(stack, nc, f"fstgc{i}", [128, 512], F16) for i in range(2)]
        xrl = [sbt(stack, nc, f"xrlc{i}", [128, 512], F32) for i in range(2)]
        pp = [pst(stack, nc, f"pp{i}", [128, 1024], F32) for i in range(2)]
        ps_hi = [pst(stack, nc, f"ps{i}", [128, 512], F32) for i in range(4, 8)]
        ps = [pp[0][:, 0:512], pp[0][:, 512:1024],
              pp[1][:, 0:512], pp[1][:, 512:1024]] + ps_hi

        p.run("vector", lambda v: v.memset(ones_row[0:1, :], 1.0))
        p.run("vector", lambda v: v.memset(ones2[0:2, :], 1.0))
        p.dma("sync", R2sb[0:2, :], r2_d[:], sem="c0")
        p.dma("sync", maskT[:], maskT_d[:], sem="c0")
        p.dma("sync", basec[:], base_d[:], sem="c0")
        p.dma("sync", ident[:], nc.dram_tensor("identd", [128, 128], BF16,
                                               kind="ExternalInput").ap()[:], sem="c0")
        p.dma("sync", k1fm[:], k1f_d[:], sem="c0")
        p.dma("sync", v1sb[:],
              AP(v1f_d.tensor, 0, [[128, 128], [128 * 128, 64], [1, 128]]), sem="c0")
        c0tot = p.val("c0")
        p.new_phase()
        for _e in ("tensor", "vector", "scalar"):
            p.wait(_e, "c0", c0tot)

        spp = 0
        pv_end = {}
        o_ps = ps[7]
        rz_ps = ps[6]
        apt_end = {}
        aex_end = {}
        for hg in range(8):
            vq = p.dma("sync", q1g[:],
                       AP(q1_d.tensor, 512 * hg * T, [[T, 128], [128 * T, 4], [1, T]]),
                       sem=f"aq{hg % 2}")
            p.wait("vector", f"aq{hg % 2}", vq)
            if hg >= 1:
                p.wait("vector", "amm", p.val("amm"))
            for hl in range(4):
                h = 4 * hg + hl
                p.run("vector", lambda v, hl=hl, h=h: v.tensor_scalar(
                    qsc[:, T * hl:T * hl + T], q1g[:, T * hl:T * hl + T],
                    SC1 / float(SLOPES[h]), None, ALU.mult), sem="aqs")
            vqs = p.val("aqs")
            p.wait("tensor", "aqs", vqs)
            for j in range(NT):
                us = spans[(j, hg)]
                nu = len(us)
                p.wait("scalar", "azr", p.val("azr"))
                first_pv = True
                for pi in range(0, nu, 2):
                    ub = us[pi:pi + 2]
                    ppar = (pi // 2) % 2
                    PTx = PTbigs[ppar]
                    for h4 in range(4):
                        h = 4 * hg + h4
                        stage = spp % 2
                        pj = pp[stage]
                        if spp >= 2:
                            p.wait("tensor", "aex", aex_end[spp - 2])

                        def fsc(t, pj=pj, h4=h4, j=j, ub=ub):
                            for k, u in enumerate(ub):
                                t.matmul(pj[:, 512 * k:512 * k + 512],
                                         ones2[0:2, 0:128],
                                         R2sb[0:2, 512 * u:512 * u + 512],
                                         start=True, stop=False)
                                r = t.matmul(
                                    pj[:, 512 * k:512 * k + 512],
                                    qsc[:, T * h4 + 128 * j:T * h4 + 128 * j + 128],
                                    k1fm[:, 512 * u:512 * u + 512],
                                    start=False, stop=True)
                            return r
                        vmm = p.run("tensor", fsc, sem="amm")
                        masked = False
                        for k, u in enumerate(ub):
                            if u >= 2 * j:
                                p.wait("vector", "amm", vmm)
                                tt_off = 512 * (u - 2 * j)
                                vmsk = p.run("vector",
                                             lambda v, pj=pj, k=k, tt_off=tt_off:
                                             v.tensor_tensor(
                                                 pj[:, 512 * k:512 * k + 512],
                                                 pj[:, 512 * k:512 * k + 512],
                                                 maskT[:, tt_off:tt_off + 512],
                                                 ALU.add), sem="amsk")
                                masked = True
                        p.wait("scalar", "amm", vmm)
                        if masked:
                            p.wait("scalar", "amsk", p.val("amsk"))
                        if spp >= 2:
                            p.wait("scalar", "apt", apt_end[spp - 2])
                        w2 = 512 * len(ub)
                        zc = zparts[:, (h4 * 10 + pi // 2):(h4 * 10 + pi // 2) + 1]
                        vex = p.run("scalar", lambda sc, pj=pj, spp=spp, h=h, zc=zc,
                                    h4=h4, j=j, w2=w2: sc.activation(
                            Psb[spp % 2][:, 0:w2], pj[:, 0:w2], AF.Exp,
                            scale=float(SLOPES[h]),
                            bias=basec[:, 8 * h + j:8 * h + j + 1],
                            accum_out=zc), sem="aex")
                        aex_end[spp] = vex
                        p.wait("tensor", "aex", vex)
                        trb = ps[4 + spp % 2]
                        p.wait("tensor", "apt", max(0, p.val("apt") - 1))
                        nch = 4 * len(ub)

                        def ftr(t, spp=spp, trb=trb, nch=nch):
                            for c in range(nch):
                                r = t.transpose(
                                    trb[:].bitcast(BF16)[:, 128 * c:128 * c + 128],
                                    Psb[spp % 2][:, 128 * c:128 * c + 128],
                                    ident[:])
                            return r
                        vtr = p.run("tensor", ftr, sem="atr")
                        p.wait("vector", "atr", vtr)
                        if spp >= 8:
                            p.wait("vector", "apv", pv_end.get((hg, j, pi // 2 - 2),
                                                              pv_end.get("last", 0)))
                        vpt = p.run("vector", lambda v, trb=trb, h4=h4, nch=nch,
                                    PTx=PTx: v.tensor_copy(
                            AP(PTx.tensor, PTx.offset + 128 * h4,
                               [[45056, 128], [512, nch], [1, 128]])
                            if PTx is not PTbig else
                            AP(PTbig.ap().tensor, 128 * h4,
                               [[4096, 128], [512, nch], [1, 128]]),
                            AP(trb[:].bitcast(BF16).tensor, 0,
                               [[1024, 128], [128, nch], [1, 128]])), sem="apt")
                        apt_end[spp] = vpt
                        spp += 1
                    vpt_all = p.val("apt")
                    p.wait("tensor", "apt", vpt_all)
                    for k, u in enumerate(ub):
                        for c in range(4):
                            tkc = 4 * u + c
                            last = (pi + k == nu - 1) and (c == 3)
                            vpv_ = p.run("tensor", lambda t, tkc=tkc, c=c, k=k,
                                  st=first_pv and k == 0 and c == 0, last=last,
                                  PTx=PTx:
                                  t.matmul(o_ps[:],
                                           v1sb[:, 128 * tkc:128 * tkc + 128],
                                           PTx[:, 512 * (4 * k + c):512 * (4 * k + c) + 512],
                                           start=st, stop=last), sem="apv")
                    pv_end[(hg, j, pi // 2)] = vpv_
                    pv_end["last"] = vpv_
                    first_pv = False
                vpv = p.val("apv")
                p.wait("vector", "apv", vpv)
                npr = (nu + 1) // 2
                vzr = p.seq("vector", [
                    lambda v, npr=npr: [v.reduce_sum(
                        zparts[:, 90 + h4:91 + h4],
                        zparts[:, 10 * h4:10 * h4 + npr], axis=AX.X)
                        for h4 in range(4)][-1],
                ], "azr")
                p.wait("sync", "azr", vzr)
                for h4 in range(4):
                    p.dma("sync", zrow[0:1, 128 * h4:128 * h4 + 128],
                          AP(zparts.ap().tensor, 90 + h4, [[96, 128], [1, 1]]),
                          sem="azc")
                vzc = p.val("azc")
                p.wait("vector", "azc", vzc)
                vrr = p.seq("vector", [
                    lambda v: v.reciprocal(zrec[0:1, 0:512], zrow[0:1, 0:512]),
                ], "azz")
                p.wait("tensor", "azz", vrr)
                vbm = p.run("tensor", lambda t: t.matmul(
                    rz_ps[:, 0:512], ones_row[0:1, :], zrec[0:1, 0:512],
                    start=True, stop=True), sem="arzm")
                p.wait("vector", "arzm", vbm)
                vev = p.seq("vector", [
                    lambda v: v.tensor_copy(rzb[:], rz_ps[:, 0:512]),
                    lambda v, hg=hg, j=j: v.tensor_tensor(
                        AP(mega.ap().tensor, 128 * (32 * hg + j),
                           [[45056, 128], [1024, 4], [1, 128]]),
                        AP(o_ps.ap().tensor, 0, [[512, 128], [128, 4], [1, 128]]),
                        AP(rzb.ap().tensor, 0, [[512, 128], [128, 4], [1, 128]]),
                        ALU.mult),
                ], "aoe")
                p.wait("tensor", "aoe", p.val("aoe"))
        p.new_phase()

        for mb in range(8):
            for fc in range(NKC):
                gidx = mb * NKC + fc
                wb = gidx % 2
                if gidx >= 2:
                    p.wait("gpsimd", "cwm", p.val("cwm") - 1)
                vw = p.dma("gpsimd", wsb[wb][:, 0:512],
                           AP(wc_d.tensor, D * 128 * fc + 512 * mb,
                              [[D, 128], [1, 512]]), sem=f"cww{wb}")
                p.wait("tensor", f"cww{wb}", vw)
                if fc == 0 and mb > 0:
                    p.wait("tensor", "cwe", p.val("cwe"))

                def fwc(t, wb=wb, fc=fc):
                    for j in range(NT):
                        r = t.matmul(ps[j][:],
                                     mega[:, 128 * (8 * fc + j):128 * (8 * fc + j) + 128],
                                     wsb[wb][:, 0:512], start=(fc == 0),
                                     stop=(fc == NKC - 1))
                    return r
                p.run("tensor", fwc, sem="cwm")
            vwm = p.val("cwm")
            for j in range(NT):
                sidx = mb * NT + j
                if sidx >= 2:
                    p.wait("sync", "cwe", p.val("cwe"))
                vx = p.dma("sync", xrl[j % 2][:],
                           x1_d[128 * j:128 * j + 128, 512 * mb:512 * mb + 512],
                           sem=f"cwx{j % 2}")
                p.wait("vector", "cwm", vwm)
                p.wait("vector", f"cwx{j % 2}", vx)
                if sidx >= 2:
                    p.wait("vector", f"cws{sidx % 2}", p.val(f"cws{sidx % 2}"))
                sg = stg[j % 2]
                vev = p.seq("vector", [
                    lambda v, j=j, sg=sg: v.tensor_tensor(
                        sg[:], ps[j][:], xrl[j % 2][:], ALU.add),
                ], "cwe")
                p.wait("gpsimd", "cwe", vev)
                p.dma("gpsimd", x2p_scr[128 * j:128 * j + 128, 512 * mb:512 * mb + 512],
                      sg[:], sem=f"cws{sidx % 2}")
        p.new_phase()

        vxn = p.dma("sync", mega[:, 0:NKC * T],
                    AP(xn1_d.tensor, 0, [[T, 128], [128 * T, NKC], [1, T]]), sem="cxn")
        p.wait("tensor", "cxn", vxn)
        for ic in range(INTER // 128):
            b = ic % 2
            if ic >= 2:
                p.wait("sync", "cgm", p.val("cgm") - 1)
            vg = p.dma("sync", wsb[b][:],
                       AP(wg_d.tensor, 128 * ic,
                          [[INTER, 128], [128 * INTER, NKC], [1, 128]]), sem=f"cgw{b}")
            vu = p.dma("sync", wsb[2 + b][:],
                       AP(wu_d.tensor, 128 * ic,
                          [[INTER, 128], [128 * INTER, NKC], [1, 128]]), sem=f"cgw{b}")
            pg = ps[0 + 4 * b:2 + 4 * b]
            pu = ps[2 + 4 * b:4 + 4 * b]
            p.wait("tensor", f"cgw{b}", vu)
            if ic >= 2:
                p.wait("tensor", "cge", p.val("cge"))

            def fgu(t, b=b, pg=pg, pu=pu):
                for kc in range(NKC):
                    for tb in range(2):
                        xs = mega[:, T * kc + 512 * tb:T * kc + 512 * tb + 512]
                        t.matmul(pg[tb][:], wsb[b][:, 128 * kc:128 * kc + 128], xs,
                                 start=(kc == 0), stop=(kc == NKC - 1))
                        r = t.matmul(pu[tb][:], wsb[2 + b][:, 128 * kc:128 * kc + 128],
                                     xs, start=(kc == 0), stop=(kc == NKC - 1))
                return r
            vm = p.run("tensor", fgu, sem="cgm")
            for tb in range(2):
                eidx = ic * 2 + tb
                p.wait("scalar", "cgm", vm)
                if eidx >= 2:
                    p.wait("scalar", "cge", p.val("cge") - 1)
                vsl = p.run("scalar", lambda sc, pg=pg, tb=tb, b=b: sc.activation(
                    bstg[tb][:, 0:512], pg[tb][:], AF.Silu), sem="csl")
                p.wait("vector", "csl", vsl)
                if eidx >= 2:
                    p.wait("vector", f"cis{eidx % 2}", p.val(f"cis{eidx % 2}"))
                vev = p.seq("vector", [
                    lambda v, pu=pu, tb=tb: v.tensor_tensor(
                        bstg[tb][:, 512:1024], bstg[tb][:, 0:512], pu[tb][:],
                        ALU.mult),
                ], "cge")
                p.wait("gpsimd", "cge", vev)
                p.dma("gpsimd", inter_scr[128 * ic:128 * ic + 128,
                                          512 * tb:512 * tb + 512],
                      bstg[tb][:, 512:1024], sem=f"cis{eidx % 2}")
        p.new_phase()

        for _b in range(2):
            p.wait("sync", f"cis{_b}", p.val(f"cis{_b}"))
        NIC = INTER // 128
        icb_sizes = [29, 29, 28]
        for th in range(2):
            if th >= 1:
                p.wait("sync", "cdm", p.val("cdm"))
            vint = p.dma("sync", mega[:, 0:NIC * 512],
                         AP(inter_scr.tensor, 512 * th,
                            [[T, 128], [128 * T, NIC], [1, 512]]), sem=f"cil{th}")
            p.wait("tensor", f"cil{th}", p.val(f"cil{th}"))
            for mc in range(NKC):
                icb0 = 0
                gidx = th * NKC + mc
                pd = ps[mc % 2]
                for bi, bsz in enumerate(icb_sizes):
                    wb = (gidx * 3 + bi) % 2
                    if gidx * 3 + bi >= 2:
                        p.wait("sync", "cdm", p.val("cdm") - 1)
                    vw = p.dma("sync", wsb[wb][:, 0:128 * bsz],
                               AP(wd_d.tensor, D * 128 * icb0 + 128 * mc,
                                  [[D, 128], [128 * D, bsz], [1, 128]]),
                               sem=f"cdw{wb}")
                    p.wait("tensor", f"cdw{wb}", vw)
                    if bi == 0 and gidx >= 2:
                        p.wait("tensor", "cde", p.val("cde"))

                    def fdn(t, wb=wb, icb0=icb0, bsz=bsz, pd=pd, bi=bi):
                        for ii in range(bsz):
                            ic = icb0 + ii
                            r = t.matmul(pd[:], wsb[wb][:, 128 * ii:128 * ii + 128],
                                         mega[:, 512 * ic:512 * ic + 512],
                                         start=(ic == 0), stop=(ic == NIC - 1))
                        return r
                    p.run("tensor", fdn, sem="cdm")
                    icb0 += bsz
                vdm = p.val("cdm")
                p.wait("vector", "cdm", vdm)
                if gidx >= 2:
                    p.wait("vector", f"cds{gidx % 2}", p.val(f"cds{gidx % 2}"))
                vev = p.seq("vector", [
                    lambda v, pd=pd, gidx=gidx: v.tensor_copy(
                        fstg[gidx % 2][:], pd[:]),
                ], "cde")
                p.wait("gpsimd", "cde", vev)
                p.dma("gpsimd", down_scr[128 * mc:128 * mc + 128,
                                         512 * th:512 * th + 512],
                      fstg[gidx % 2][:], sem=f"cds{gidx % 2}")
        p.new_phase()

        for _b in range(2):
            p.wait("sync", f"cds{_b}", p.val(f"cds{_b}"))
            p.wait("sync", f"cws{_b}", p.val(f"cws{_b}"))
        for j in range(NT):
            for mb in range(8):
                sidx = j * 8 + mb
                bx = xrl[sidx % 2]
                bd = bstg[sidx % 2][:, 0:512].bitcast(F16)[:, 0:512]
                if sidx >= 2:
                    p.wait("sync", "cfe", p.val("cfe"))
                vx = p.dma("sync", bx[:],
                           x2p_scr[128 * j:128 * j + 128, 512 * mb:512 * mb + 512],
                           sem=f"cfx{sidx % 2}")
                vd = p.dma("sync", bd,
                           down_scr[512 * mb:512 * mb + 512, 128 * j:128 * j + 128],
                           sem=f"cfx{sidx % 2}", transpose=True)
                p.wait("vector", f"cfx{sidx % 2}", vd)
                if sidx >= 2:
                    p.wait("vector", f"cfs{sidx % 2}", p.val(f"cfs{sidx % 2}"))
                sg = stg[sidx % 2]
                vev = p.seq("vector", [
                    lambda v, sg=sg, bx=bx, bd=bd: v.tensor_tensor(
                        sg[:], bx[:], bd, ALU.add),
                ], "cfe")
                p.wait("gpsimd", "cfe", vev)
                p.dma("gpsimd", out_d[128 * j:128 * j + 128, 512 * mb:512 * mb + 512],
                      sg[:], sem=f"cfs{sidx % 2}")
        p.new_phase()
        for _b in range(2):
            p.wait("gpsimd", f"cfs{_b}", p.val(f"cfs{_b}"))
        p.emit()
    return nc


def prep_launch_C(inputs, a_results, b_results):
    k1f = np.ascontiguousarray(unstripe([np.asarray(r["k1"]) for r in b_results]).T)
    v1f = unstripe([r["v1"] for r in b_results])
    wg = np.concatenate([r["wgsl"] for r in a_results], axis=1)
    wu = np.concatenate([r["wusl"] for r in a_results], axis=1)
    wd = np.concatenate([r["wdsl"] for r in a_results], axis=1)
    ident = np.eye(128, dtype=bf16)
    ramp = np.arange(512, dtype=np.float32).reshape(1, 512)
    upr = np.arange(8192)
    r2 = np.stack([(upr % 256).astype(bf16), (upr - upr % 256).astype(bf16)])
    maps = []
    for r in range(NCORES):
        basec = np.zeros((128, 256), np.float32)
        for h in range(32):
            for j in range(NT):
                posq = 128.0 * (r + 8 * j) + np.arange(128, dtype=np.float32)
                basec[:, 8 * h + j] = -float(SLOPES[h]) * posq - M1
        maskT = np.zeros((128, 1024), np.float32)
        for t in range(2):
            pk = 512 * t + np.arange(512, dtype=np.float32)[None, :]
            pq = 128.0 * r + np.arange(128, dtype=np.float32)[:, None]
            maskT[:, 512 * t:512 * t + 512] = np.where(pk <= pq, 0.0, -2e30)
        maps.append({
            "q1": b_results[r]["q1"], "k1fT": k1f, "v1f": v1f,
            "x1": b_results[r]["x1"], "xn1": b_results[r]["xn1"],
            "wg": wg, "wu": wu, "wd": wd, "wc": inputs["L1_attn_w_c"],
            "ramp": ramp, "r2rows": r2, "basecols": basec, "maskT": maskT,
            "identd": ident,
        })
    return maps



def kernel(**inputs):
    mapsA = prep_launch_A(inputs)
    ncA = build_launch_A()
    resA = run_bass_kernel_spmd(ncA, mapsA, core_ids=list(range(NCORES))).results
    mapsB = prep_launch_B(inputs, resA)
    ncB = build_launch_B()
    resB = run_bass_kernel_spmd(ncB, mapsB, core_ids=list(range(NCORES))).results
    mapsC = prep_launch_C(inputs, resA, resB)
    ncC = build_launch_C()
    resC = run_bass_kernel_spmd(ncC, mapsC, core_ids=list(range(NCORES))).results
    out = unstripe([np.asarray(r["outp"], np.float32) for r in resC])
    return out.reshape(1, S, D)
